# version 08
# baseline (speedup 1.0000x reference)
import math
import numpy as np
from ml_dtypes import bfloat16
from contextlib import ExitStack

import concourse.bass as bass
import concourse.mybir as mybir
from concourse.bass_utils import run_bass_kernel_spmd

F32 = mybir.dt.float32
BF16 = mybir.dt.bfloat16
AF = mybir.ActivationFunctionType
OP = mybir.AluOpType
AX = mybir.AxisListType

D = 2048
L = 2048
NH = 16
DH = 128
NB = 16
BS = 512
JD = 8192
NT = L // 128
NC = D // 128
INV_SCALE = 1.0 / math.sqrt(128.0)

ENGS = ("tensor", "vector", "scalar", "gpsimd", "sync")


class Prog:
    def __init__(self, nc, es):
        self.nc = nc
        self.es = es
        self.st = {k: [] for k in ENGS}
        self.sems = {}
        self.val = {}

    NSL = 4

    def sem(self, name):
        self.sems[name] = self.es.enter_context(self.nc.semaphore(name))
        self.val[name] = 0
        return name

    def dsem(self, name):
        for i in range(self.NSL):
            self.sem(f"{name}{i}")
        self.val[name + "_i"] = 0
        return name

    def dma(self, eng, fn, q):
        i = self.val[q + "_i"] % self.NSL
        self.val[q + "_i"] += 1
        sub = f"{q}{i}"
        self.w(eng, sub, self.val[sub])
        self.val[sub] += 16
        h = self.sems[sub]
        self.st[eng].append(lambda e, fn=fn, h=h: fn(e).then_inc(h, 16))
        return (sub, self.val[sub])

    def wd(self, eng, tok):
        if tok is not None:
            self.w(eng, tok[0], tok[1])

    def dbar(self, eng, q):
        for i in range(self.NSL):
            sub = f"{q}{i}"
            self.w(eng, sub, self.val[sub])

    SELF = {"vector": "cV", "scalar": "cS", "gpsimd": "cG"}

    def w(self, eng, sem, v=None):
        vv = self.val[sem] if v is None else v
        h = self.sems[sem]
        self.st[eng].append(lambda e, h=h, vv=vv: e.wait_ge(h, vv))

    def e(self, eng, fn, inc=None, k=1, dma=False):
        if not dma and eng in self.SELF:
            own = self.SELF[eng]
            self.w(eng, own, self.val[own])
            inc, k = own, 1
        if inc is not None:
            h = self.sems[inc]
            self.val[inc] += k
            self.st[eng].append(lambda e, fn=fn, h=h, k=k: fn(e).then_inc(h, k))
        else:
            self.st[eng].append(lambda e, fn=fn: fn(e))

    def v(self, sem):
        return self.val[sem]


def build(nc):
    es = ExitStack()
    p = Prog(nc, es)
    x_d = nc.dram_tensor("x", [L, D], F32, kind="ExternalInput").ap()
    wk_d = nc.dram_tensor("wk", [D, D], F32, kind="ExternalInput").ap()
    wv_d = nc.dram_tensor("wv", [D, D], F32, kind="ExternalInput").ap()
    wo_d = nc.dram_tensor("wo", [D, D], F32, kind="ExternalInput").ap()
    wqr_d = nc.dram_tensor("wqr", [NH, 128, D], F32, kind="ExternalInput").ap()
    wur_d = nc.dram_tensor("wur", [NH, 128, D], F32, kind="ExternalInput").ap()
    w1r_d = nc.dram_tensor("w1r", [JD // 128, 128, D], F32, kind="ExternalInput").ap()
    w2t_d = nc.dram_tensor("w2t", [JD, D], F32, kind="ExternalInput").ap()
    wrr_d = nc.dram_tensor("wrr", [128, NC * NB], F32, kind="ExternalInput").ap()
    br_d = nc.dram_tensor("brd", [1, NB], F32, kind="ExternalInput").ap()
    cst_d = nc.dram_tensor("cst", [128, 2], F32, kind="ExternalInput").ap()
    idnbf_d = nc.dram_tensor("idnbf", [128, 128], BF16, kind="ExternalInput").ap()
    onescol_d = nc.dram_tensor("onescol", [128, 1], BF16, kind="ExternalInput").ap()
    onesrb_d = nc.dram_tensor("onesrb", [1, 128], BF16, kind="ExternalInput").ap()
    onesrf_d = nc.dram_tensor("onesrf", [1, 128], F32, kind="ExternalInput").ap()
    out_d = nc.dram_tensor("out", [L, D], F32, kind="ExternalOutput").ap()
    g_d = nc.dram_tensor("gscr", [L, D], F32, kind="Internal").ap()

    dS = p.dsem("dS")
    dG = p.dsem("dG")
    dO = p.dsem("dO")
    dV = p.dsem("dV")
    cT = p.sem("cT")
    cV = p.sem("cV")
    cS = p.sem("cS")
    cG = p.sem("cG")

    A = es.enter_context(nc.sbuf_tensor("A", [128, NC * L], BF16))
    kvsb = es.enter_context(nc.sbuf_tensor("kvsb", [128, NH * DH], BF16))
    maskT = es.enter_context(nc.sbuf_tensor("maskT", [128, L], BF16))
    idnbf = es.enter_context(nc.sbuf_tensor("idnbfs", [128, 128], BF16))
    onescol = es.enter_context(nc.sbuf_tensor("onescols", [128, 1], BF16))
    onesrb = es.enter_context(nc.sbuf_tensor("onesrbs", [1, 128], BF16))
    onesrf = es.enter_context(nc.sbuf_tensor("onesrfs", [1, 128], F32))
    wrthi = es.enter_context(nc.sbuf_tensor("wrthi", [128, NC * NB], BF16))
    wrtlo = es.enter_context(nc.sbuf_tensor("wrtlo", [128, NC * NB], BF16))
    brbc = es.enter_context(nc.sbuf_tensor("brbc", [128, NB], F32))
    br1 = es.enter_context(nc.sbuf_tensor("br1", [1, NB], F32))
    cst = es.enter_context(nc.sbuf_tensor("cstsb", [128, 2], F32))
    nc.const_aps.register(F32, 1e-5, cst[:, 0:1])
    nc.const_aps.register(F32, INV_SCALE, cst[:, 1:2])

    for dst, src in ((idnbf, idnbf_d), (onescol, onescol_d),
                     (onesrb, onesrb_d), (onesrf, onesrf_d),
                     (br1, br_d), (cst, cst_d)):
        p.dma("gpsimd", lambda e, dst=dst, src=src: e.dma_start(out=dst[:], in_=src[:]), dG)


    p0 = ExitStack()
    wrt32 = p0.enter_context(nc.sbuf_tensor("wrt32", [128, NC * NB], F32))
    p.dma("gpsimd", lambda e: e.dma_start(out=wrt32[:], in_=wrr_d[:]), dG)
    p.dbar("vector", dG)
    p.e("vector", lambda e: e.tensor_copy(wrthi[:], wrt32[:]), inc=cV)
    p.e("vector", lambda e: e.tensor_tensor(wrtlo[:], wrt32[:], wrthi[:], OP.subtract), inc=cV)
    wrt_ready = p.v(cV)
    xtile = [p0.enter_context(nc.sbuf_tensor(f"xtile{i}", [128, D], F32)) for i in range(2)]
    xn32 = p0.enter_context(nc.sbuf_tensor("xn32", [128, D], F32))
    xnhi = p0.enter_context(nc.sbuf_tensor("xnhi", [128, D], BF16))
    xnlo = p0.enter_context(nc.sbuf_tensor("xnlo", [128, D], BF16))
    sqsc = p0.enter_context(nc.sbuf_tensor("sqsc", [128, D], F32))
    sumx = p0.enter_context(nc.sbuf_tensor("sumx", [128, 1], F32))
    ssq = p0.enter_context(nc.sbuf_tensor("ssq", [128, 1], F32))
    mmean = p0.enter_context(nc.sbuf_tensor("mmean", [128, 1], F32))
    m2 = p0.enter_context(nc.sbuf_tensor("m2", [128, 1], F32))
    varb = p0.enter_context(nc.sbuf_tensor("varb", [128, 1], F32))
    stdb = p0.enter_context(nc.sbuf_tensor("stdb", [128, 1], F32))
    rinv = p0.enter_context(nc.sbuf_tensor("rinv", [128, 1], F32))
    alo = [p0.enter_context(nc.sbuf_tensor(f"alo{i}", [128, 128], BF16)) for i in range(2)]
    logits = p0.enter_context(nc.sbuf_tensor("logits", [128, NB], F32))
    cmp0 = p0.enter_context(nc.sbuf_tensor("cmp0", [128, NB], F32))
    cnt = p0.enter_context(nc.sbuf_tensor("cnt", [128, NB], F32))
    maskb = p0.enter_context(nc.sbuf_tensor("maskb", [128, NB], BF16))
    pt = [p0.enter_context(nc.psum_tensor(f"pt{i}", [128, 128], BF16)) for i in range(2)]
    ptl = [p0.enter_context(nc.psum_tensor(f"ptl{i}", [128, 128], BF16)) for i in range(2)]
    plog = p0.enter_context(nc.psum_tensor("plog", [128, 512], F32))
    pmt = p0.enter_context(nc.psum_tensor("pmt", [128, 128], BF16))

    xdma_done = [None] * NT
    ln_done = [0] * NT
    lnhi_done = [0] * NT
    tr_done = [[0] * NC for _ in range(NT)]
    ahi_done = [[0] * NC for _ in range(NT)]
    alo_done = [[0] * NC for _ in range(NT)]
    rt_done = [0] * NT
    mask_done = [0] * NT

    def p0_ln(t):
        if t >= 2:
            p.w("gpsimd", cV, ln_done[t - 2])
        xdma_done[t] = p.dma("gpsimd", lambda e, t=t: e.dma_start(out=xtile[t % 2][:], in_=x_d[t * 128:(t + 1) * 128, :]), dG)
        p.wd("vector", xdma_done[t])
        p.e("vector", lambda e, t=t: e.reduce_sum(sumx[:], xtile[t % 2][:], axis=AX.X), inc=cV)
        p.wd("scalar", xdma_done[t])
        if t >= 1:
            p.w("scalar", cV, ln_done[t - 1])
        p.e("scalar", lambda e, t=t: e.activation(out=sqsc[:], in_=xtile[t % 2][:], func=AF.Square, accum_out=ssq[:, 0:1]), inc=cS)
        sq_done = p.v(cS)
        p.e("vector", lambda e: e.tensor_scalar(out=mmean[:], in0=sumx[:], scalar1=1.0 / D, scalar2=None, op0=OP.mult))
        p.e("vector", lambda e: e.tensor_tensor(out=m2[:], in0=mmean[:], in1=mmean[:], op=OP.mult))
        p.w("vector", cS, sq_done)
        p.e("vector", lambda e: e.tensor_scalar(out=varb[:], in0=ssq[:], scalar1=1.0 / D, scalar2=m2[:, 0:1], op0=OP.mult, op1=OP.subtract), inc=cV)
        var_done = p.v(cV)
        p.w("scalar", cV, var_done)
        p.e("scalar", lambda e: e.activation(out=stdb[:], in_=varb[:], func=AF.Sqrt, bias=1e-5), inc=cS)
        std_done = p.v(cS)
        p.w("vector", cS, std_done)
        p.e("vector", lambda e: e.reciprocal(out=rinv[:], in_=stdb[:]))
        if t >= 1:
            p.w("vector", cT, tr_done[t - 1][NC - 1])
        p.e("vector", lambda e, t=t: e.tensor_scalar(out=xn32[:], in0=xtile[t % 2][:], scalar1=mmean[:, 0:1], scalar2=rinv[:, 0:1], op0=OP.subtract, op1=OP.mult), inc=cV)
        xn_done = p.v(cV)
        p.w("scalar", cV, xn_done)
        p.e("scalar", lambda e: e.copy(out=xnhi[:], in_=xn32[:]), inc=cS)
        hi_done = p.v(cS)
        p.w("vector", cS, hi_done)
        p.e("vector", lambda e: e.tensor_tensor(out=xnlo[:], in0=xn32[:], in1=xnhi[:], op=OP.subtract), inc=cV)
        ln_done[t] = p.v(cV)
        lnhi_done[t] = hi_done

    def p0_tr(t):
        p.w("tensor", cS, lnhi_done[t])
        p.w("tensor", cV, ln_done[t])
        for c in range(NC):
            slot = pt[c % 2]
            if c >= 2:
                p.w("tensor", cS, ahi_done[t][c - 2])
                p.w("tensor", cV, alo_done[t][c - 2])
            p.e("tensor", lambda e, t=t, c=c, slot=slot: e.transpose(slot[:, 0:128], xnhi[:, c * 128:(c + 1) * 128], idnbf[:]))
            p.e("tensor", lambda e, t=t, c=c, lslot=ptl[c % 2]: e.transpose(lslot[:, 0:128], xnlo[:, c * 128:(c + 1) * 128], idnbf[:]), inc=cT)
            tr_done[t][c] = p.v(cT)
            p.w("scalar", cT, tr_done[t][c])
            p.e("scalar", lambda e, t=t, c=c, slot=slot: e.copy(out=A[:, c * L + t * 128: c * L + (t + 1) * 128], in_=slot[:, 0:128]), inc=cS)
            ahi_done[t][c] = p.v(cS)
            if c >= 2:
                p.w("vector", cT, rt_done_c[c - 2])
            p.w("vector", cT, tr_done[t][c])
            p.e("vector", lambda e, t=t, c=c, lslot=ptl[c % 2]: e.tensor_copy(alo[c % 2][:], lslot[:, 0:128]), inc=cV)
            alo_done[t][c] = p.v(cV)
            p.w("tensor", cS, ahi_done[t][c])
            p.w("tensor", cV, alo_done[t][c])
            st = (c == 0)
            p.e("tensor", lambda e, t=t, c=c, st=st: e.matmul(plog[:, 0:NB], A[:, c * L + t * 128: c * L + (t + 1) * 128], wrthi[:, c * NB:(c + 1) * NB], start=st, stop=False))
            p.e("tensor", lambda e, t=t, c=c: e.matmul(plog[:, 0:NB], A[:, c * L + t * 128: c * L + (t + 1) * 128], wrtlo[:, c * NB:(c + 1) * NB], start=False, stop=False))
            p.e("tensor", lambda e, t=t, c=c: e.matmul(plog[:, 0:NB], alo[c % 2][:], wrthi[:, c * NB:(c + 1) * NB], start=False, stop=(c == NC - 1)), inc=cT)
            rt_done_c[c] = p.v(cT)
        rt_done[t] = p.v(cT)
        p.w("vector", cT, rt_done[t])
        p.w("vector", cS, brbc_done)
        p.e("vector", lambda e: e.tensor_tensor(out=logits[:], in0=plog[:, 0:NB], in1=brbc[:], op=OP.add), inc=cV)
        for i in range(NB):
            p.e("vector", lambda e, i=i: e.tensor_scalar(out=cmp0[:], in0=logits[:], scalar1=logits[:, i:i + 1], scalar2=0.0, op0=OP.is_gt, op1=OP.add, accum_out=cnt[:, i:i + 1]))
        p.e("vector", lambda e: e.tensor_scalar(out=maskb[:], in0=cnt[:], scalar1=3.5, scalar2=None, op0=OP.is_lt), inc=cV)
        mk_done = p.v(cV)
        p.w("tensor", cV, mk_done)
        if t >= 1:
            p.w("tensor", cS, mask_done[t - 1])
        p.e("tensor", lambda e: e.transpose(pmt[0:NB, 0:128], maskb[:], idnbf[:]), inc=cT)
        mt_done = p.v(cT)
        p.w("scalar", cT, mt_done)
        p.e("scalar", lambda e, t=t: e.copy(out=maskT[0:NB, t * 128:(t + 1) * 128], in_=pmt[0:NB, 0:128]), inc=cS)
        mask_done[t] = p.v(cS)

    p.w("tensor", cV, wrt_ready)
    p.dbar("tensor", dG)
    p.e("tensor", lambda e: e.matmul(plog[:, 0:NB], onesrf[0:1, :], br1[0:1, :], start=True, stop=True), inc=cT)
    brmm = p.v(cT)
    p.w("scalar", cT, brmm)
    p.e("scalar", lambda e: e.copy(out=brbc[:], in_=plog[:, 0:NB]), inc=cS)
    brbc_done = p.v(cS)
    p.w("tensor", cS, brbc_done)
    rt_done_c = [0] * NC
    for t in range(NT):
        p0_ln(t)
        p0_tr(t)
    p0_end_T = p.v(cT)
    p0_end_S = p.v(cS)
    p0_end_V = p.v(cV)
    p0.close()
    p.w("sync", cT, p0_end_T)
    p.w("sync", cS, p0_end_S)
    p.w("sync", cV, p0_end_V)

    p1 = ExitStack()
    wring = [p1.enter_context(nc.sbuf_tensor(f"wring{i}", [128, 512], F32)) for i in range(4)]
    wkb = [p1.enter_context(nc.sbuf_tensor(f"wkb{i}", [128, NC * 512], BF16)) for i in range(2)]
    wvb = [p1.enter_context(nc.sbuf_tensor(f"wvb{i}", [128, NC * 512], BF16)) for i in range(2)]
    ksb = [p1.enter_context(nc.sbuf_tensor(f"ksb{i}", [128, 512], BF16)) for i in range(2)]
    vsb = [p1.enter_context(nc.sbuf_tensor(f"vsb{i}", [128, 512], BF16)) for i in range(2)]
    kp = [p1.enter_context(nc.psum_tensor(f"kp{i}", [128, 512], F32)) for i in range(2)]
    vp = [p1.enter_context(nc.psum_tensor(f"vp{i}", [128, 512], F32)) for i in range(2)]
    kvp = p1.enter_context(nc.psum_tensor("kvp", [128, NH * DH], F32))

    wcast_done = [0, 0]
    wcons_done = [0, 0]
    kevac = [0, 0]
    vevac = [0, 0]
    kvmm = [0, 0]

    for ncx in range(4):
        par = ncx % 2
        for i in range(NC):
            rs = (ncx * NC + i) % 4
            p.w("sync", cS, wring_cast[rs])
            dd = p.dma("sync", lambda e, i=i, ncx=ncx, rs=rs: e.dma_start(out=wring[rs][:], in_=wk_d[i * 128:(i + 1) * 128, ncx * 512:(ncx + 1) * 512]), dS)
            p.wd("scalar", dd)
            if ncx >= 2 and i == 0:
                p.w("scalar", cT, wcons_done[par])
            p.e("scalar", lambda e, i=i, par=par, rs=rs: e.copy(out=wkb[par][:, i * 512:(i + 1) * 512], in_=wring[rs][:]), inc=cS)
            wring_cast[rs] = p.v(cS)
        for i in range(NC):
            rs = (ncx * NC + i) % 4
            p.w("sync", cS, wring_cast[rs])
            dd = p.dma("sync", lambda e, i=i, ncx=ncx, rs=rs: e.dma_start(out=wring[rs][:], in_=wv_d[i * 128:(i + 1) * 128, ncx * 512:(ncx + 1) * 512]), dS)
            p.wd("scalar", dd)
            p.e("scalar", lambda e, i=i, par=par, rs=rs: e.copy(out=wvb[par][:, i * 512:(i + 1) * 512], in_=wring[rs][:]), inc=cS)
            wring_cast[rs] = p.v(cS)
        wcast_done[par] = p.v(cS)
        p.w("tensor", cS, wcast_done[par])

        kevac_t = [0] * NT
        vevac_t = [0] * NT
        kv_t = [0] * NT
        for t in range(NT):
            sl = t % 2
            if t >= 2:
                p.w("tensor", cS, kevac_t[t - 2])
            for i in range(NC):
                p.e("tensor", lambda e, i=i, t=t, par=par, sl=sl: e.matmul(kp[sl][:], A[:, i * L + t * 128: i * L + (t + 1) * 128], wkb[par][:, i * 512:(i + 1) * 512], start=(i == 0), stop=(i == NC - 1)), inc=(cT if i == NC - 1 else None))
            kmm = p.v(cT)
            if t >= 2:
                p.w("tensor", cV, vevac_t[t - 2])
            for i in range(NC):
                p.e("tensor", lambda e, i=i, t=t, par=par, sl=sl: e.matmul(vp[sl][:], A[:, i * L + t * 128: i * L + (t + 1) * 128], wvb[par][:, i * 512:(i + 1) * 512], start=(i == 0), stop=(i == NC - 1)), inc=(cT if i == NC - 1 else None))
            vmm = p.v(cT)
            p.w("scalar", cT, kmm)
            if t >= 2:
                p.w("scalar", cT, kv_t[t - 2])
            p.e("scalar", lambda e, sl=sl: e.activation(out=ksb[sl][:], in_=kp[sl][:], func=AF.Relu, scale=INV_SCALE), inc=cS)
            kevac_t[t] = p.v(cS)
            p.w("vector", cT, vmm)
            if t >= 2:
                p.w("vector", cT, kv_t[t - 2])
            p.e("vector", lambda e, sl=sl: e.tensor_copy(vsb[sl][:], vp[sl][:]), inc=cV)
            vevac_t[t] = p.v(cV)
            p.w("tensor", cS, kevac_t[t])
            p.w("tensor", cV, vevac_t[t])
            for hh in range(4):
                h = ncx * 4 + hh
                p.e("tensor", lambda e, h=h, hh=hh, sl=sl, t=t: e.matmul(kvp[:, h * 128:(h + 1) * 128], ksb[sl][:, hh * 128:(hh + 1) * 128], vsb[sl][:, hh * 128:(hh + 1) * 128], start=(t == 0 and hh == 0), stop=(t == NT - 1 and hh == 3)), inc=(cT if hh == 3 else None))
            kv_t[t] = p.v(cT)
        wcons_done[par] = p.v(cT)
        p.w("vector", cT, wcons_done[par])
        p.e("vector", lambda e, ncx=ncx: e.tensor_copy(kvsb[:, ncx * 512:(ncx + 1) * 512], kvp[:, ncx * 512:(ncx + 1) * 512]), inc=cV)
    kv_ready = p.v(cV)
    p1_end_T = p.v(cT)
    p1_end_S = p.v(cS)
    p1.close()
    p.w("sync", cT, p1_end_T)
    p.w("sync", cS, p1_end_S)
    p.w("sync", cV, kv_ready)

    p3 = ExitStack()
    y_acc = p3.enter_context(nc.sbuf_tensor("y_acc", [128, 8 * D], F32))
    hm = p3.enter_context(nc.sbuf_tensor("hm", [128, 8 * 1024], BF16))
    w2b = [p3.enter_context(nc.sbuf_tensor(f"w2b{i}", [128, 8 * 1024], BF16)) for i in range(2)]
    w1ring = [p3.enter_context(nc.sbuf_tensor(f"w1ring{i}", [128, 1024], F32)) for i in range(2)]
    w1b = [p3.enter_context(nc.sbuf_tensor(f"w1b{i}", [128, 1024], BF16)) for i in range(2)]
    w2ring = [p3.enter_context(nc.sbuf_tensor(f"w2ring{i}", [128, 512], F32)) for i in range(2)]
    mbc = [p3.enter_context(nc.sbuf_tensor(f"mbc{i}", [128, 1024], BF16)) for i in range(2)]
    mrow = p3.enter_context(nc.sbuf_tensor("mrow0", [1, 512], BF16))
    hps = [p3.enter_context(nc.psum_tensor(f"hps{i}", [128, 1024], F32)) for i in range(2)]
    yps = [p3.enter_context(nc.psum_tensor(f"yps{i}", [128, 1024], F32)) for i in range(2)]

    w1dma = [0, 0]
    w1cast = [0, 0]
    w1cons = [0, 0]
    gelu_done = {}
    hmmask_done = {}
    gst_dma = [None, None]
    w2cast_slot = [0, 0]
    w2cast_slot_g = [0, 0]
    w2cons_fh = [0, 0]
    yadd_slot = [0, 0]
    mrow_guard = [0, 0]
    silu_mark = 0

    p.w("tensor", cV, kv_ready)
    for Hh in range(2):
        tok0 = Hh * 1024
        for slab in range(8):
            b0 = slab * 2
            if (Hh, slab) != (0, 0):
                p.w("tensor", cS, gelu_done[(Hh, slab - 1, 7) if slab > 0 else (Hh - 1, 7, 7)])
            p.w("tensor", cS, mask_done[NT - 1])
            for bb in range(2):
                b = b0 + bb
                if bb == 1:
                    p.w("tensor", cS, p.v(cS))
                for gg in range(2):
                    p.w("scalar", cS, mask_done[NT - 1])
                    p.w("scalar", cT, mrow_guard[0])
                    mrow_dma = p.dma("scalar", lambda e, b=b, gg=gg, tok0=tok0: e.dma_start(out=mrow[0:1, :], in_=maskT[b:b + 1, tok0 + gg * 512: tok0 + (gg + 1) * 512]), dV)
                    p.wd("tensor", mrow_dma)
                    p.e("tensor", lambda e, gg=gg: e.matmul(hps[0][:, gg * 512:(gg + 1) * 512], onesrb[0:1, :], mrow[0:1, :], start=True, stop=True), inc=cT)
                    mrow_guard[0] = p.v(cT)
                bcmm = p.v(cT)
                p.w("scalar", cT, bcmm)
                p.e("scalar", lambda e, bb=bb: e.copy(out=mbc[bb][:], in_=hps[0][:, 0:1024]), inc=cS)
            mbc_done = p.v(cS)
            p.w("tensor", cS, mbc_done)

            for jc in range(8):
                jg = slab * 8 + jc
                sl = jc % 2
                for hf in range(2):
                    p.w("sync", cS, w1cast[hf])
                    w1dma[hf] = p.dma("sync", lambda e, jg=jg, hf=hf: e.dma_start(out=w1ring[hf][:], in_=w1r_d[jg, :, hf * 1024:(hf + 1) * 1024]), dS)
                    p.wd("scalar", w1dma[hf])
                    p.w("scalar", cT, w1cons[hf])
                    p.e("scalar", lambda e, hf=hf: e.copy(out=w1b[hf][:], in_=w1ring[hf][:]), inc=cS)
                    w1cast[hf] = p.v(cS)
                if (Hh * 64 + jg) >= 1:
                    p.w("tensor", cS, gelu_done.get((Hh, slab, jc - 2), gelu_done.get((Hh, slab - 1, jc + 6), 0)))
                for hf in range(2):
                    p.w("tensor", cS, w1cast[hf])
                    for ng in range(2):
                        for ii in range(8):
                            i = hf * 8 + ii
                            p.e("tensor", lambda e, i=i, ii=ii, ng=ng, sl=sl, hf=hf, tok0=tok0: e.matmul(hps[sl][:, ng * 512:(ng + 1) * 512], w1b[hf][:, ii * 128:(ii + 1) * 128], A[:, i * L + tok0 + ng * 512: i * L + tok0 + (ng + 1) * 512], start=(i == 0), stop=(i == NC - 1)), inc=(cT if (i == NC - 1 and ng == 1) or (ii == 7 and ng == 1) else None))
                    w1cons[hf] = p.v(cT)
                hmm = p.v(cT)
                p.w("scalar", cT, hmm)
                p.e("scalar", lambda e, jc=jc, sl=sl: e.activation(out=hm[:, jc * 1024:(jc + 1) * 1024], in_=hps[sl][:], func=AF.Gelu), inc=cS)
                gelu_done[(Hh, slab, jc)] = p.v(cS)
                p.w("vector", cS, gelu_done[(Hh, slab, jc)])
                p.w("vector", cS, mbc_done)
                p.e("vector", lambda e, jc=jc, bb=jc // 4: e.tensor_tensor(out=hm[:, jc * 1024:(jc + 1) * 1024], in0=hm[:, jc * 1024:(jc + 1) * 1024], in1=mbc[bb][:], op=OP.mult), inc=cV)
                hmmask_done[(Hh, slab, jc)] = p.v(cV)
            hm_ready = p.v(cV)

            for fh in range(2):
                qeng = "sync" if fh == 0 else "gpsimd"
                qsem = dS if fh == 0 else dG
                for jc in range(8):
                    jg = slab * 8 + jc
                    for cc in range(2):
                        p.w(qeng, cS, w2cast_slot[cc])
                        dd = p.dma(qeng, lambda e, jg=jg, fh=fh, cc=cc: e.dma_start(out=w2ring[cc][:], in_=w2t_d[jg * 128:(jg + 1) * 128, fh * 1024 + cc * 512: fh * 1024 + (cc + 1) * 512]), qsem)
                        p.wd("scalar", dd)
                        if jc == 0 and cc == 0:
                            p.w("scalar", cT, w2cons_fh[fh])
                        p.e("scalar", lambda e, jc=jc, fh=fh, cc=cc: e.copy(out=w2b[fh][:, jc * 1024 + cc * 512: jc * 1024 + (cc + 1) * 512], in_=w2ring[cc][:]), inc=cS)
                        w2cast_slot[cc] = p.v(cS)
                w2_ready = p.v(cS)
                p.w("tensor", cV, hm_ready)
                p.w("tensor", cS, w2_ready)
                for t8 in range(8):
                    sl = t8 % 2
                    p.w("tensor", cV, yadd_slot[sl])
                    for jc in range(8):
                        for fg in range(2):
                            p.e("tensor", lambda e, t8=t8, jc=jc, fg=fg, fh=fh, sl=sl: e.matmul(yps[sl][:, fg * 512:(fg + 1) * 512], hm[:, jc * 1024 + t8 * 128: jc * 1024 + (t8 + 1) * 128], w2b[fh][:, jc * 1024 + fg * 512: jc * 1024 + (fg + 1) * 512], start=(jc == 0), stop=(jc == 7)), inc=(cT if (jc == 7 and fg == 1) else None))
                    ymm = p.v(cT)
                    p.w("vector", cT, ymm)
                    if slab == 0:
                        if Hh == 1:
                            p.w("vector", cS, silu_mark)
                            p.dbar("vector", dG)
                        p.e("vector", lambda e, t8=t8, fh=fh, sl=sl: e.tensor_copy(y_acc[:, t8 * D + fh * 1024: t8 * D + (fh + 1) * 1024], yps[sl][:]), inc=cV)
                    else:
                        p.e("vector", lambda e, t8=t8, fh=fh, sl=sl: e.tensor_tensor(out=y_acc[:, t8 * D + fh * 1024: t8 * D + (fh + 1) * 1024], in0=y_acc[:, t8 * D + fh * 1024: t8 * D + (fh + 1) * 1024], in1=yps[sl][:], op=OP.add), inc=cV)
                    yadd_slot[sl] = p.v(cV)
                w2cons_fh[fh] = p.v(cT)
        yfin = p.v(cV)
        for t8 in range(8):
            p.w("scalar", cV, yfin)
            p.e("scalar", lambda e, t8=t8: e.activation(out=y_acc[:, t8 * D:(t8 + 1) * D], in_=y_acc[:, t8 * D:(t8 + 1) * D], func=AF.Silu), inc=cS)
            gdone = p.v(cS)
            p.w("gpsimd", cS, gdone)
            gst_dma[0] = p.dma("gpsimd", lambda e, t8=t8, tok0=tok0: e.dma_start(out=g_d[tok0 + t8 * 128: tok0 + (t8 + 1) * 128, :], in_=y_acc[:, t8 * D:(t8 + 1) * D]), dG)
        silu_mark = p.v(cS)
    p3.close()

    p2o = ExitStack()
    osb = p2o.enter_context(nc.sbuf_tensor("osb", [128, NH * L], BF16))
    p2 = ExitStack()
    wring2 = [p2.enter_context(nc.sbuf_tensor(f"wring2{i}", [128, 512], F32)) for i in range(2)]
    wqb = [p2.enter_context(nc.sbuf_tensor(f"wqb{i}", [128, D], BF16)) for i in range(2)]
    wub = [p2.enter_context(nc.sbuf_tensor(f"wub{i}", [128, D], BF16)) for i in range(2)]
    qtsb = [p2.enter_context(nc.sbuf_tensor(f"qtsb{i}", [128, L], BF16)) for i in range(2)]
    utsb = [p2.enter_context(nc.sbuf_tensor(f"utsb{i}", [128, L], BF16)) for i in range(2)]
    attnsb = [p2.enter_context(nc.sbuf_tensor(f"attnsb{i}", [128, L], F32)) for i in range(2)]
    sq32 = p2.enter_context(nc.sbuf_tensor("sq32", [128, 1024], F32))
    sqh = p2.enter_context(nc.sbuf_tensor("sqh", [128, 1024], BF16))
    sql = p2.enter_context(nc.sbuf_tensor("sql", [128, 1024], BF16))
    srow = p2.enter_context(nc.sbuf_tensor("srow", [1, 512], F32))
    srow2 = p2.enter_context(nc.sbuf_tensor("srow2", [1, 512], F32))
    tmp512 = p2.enter_context(nc.sbuf_tensor("tmp512", [128, 512], F32))
    qp = [p2.enter_context(nc.psum_tensor(f"qp{i}", [128, 512], F32)) for i in range(2)]
    ap_ = [p2.enter_context(nc.psum_tensor(f"ap{i}", [128, 512], F32)) for i in range(2)]
    ssp = p2.enter_context(nc.psum_tensor("ssp", [128, 512], F32))
    bcp = [p2.enter_context(nc.psum_tensor(f"bcp{i}", [128, 512], F32)) for i in range(2)]

    wq_cast = [0, 0]
    wq_cons = [0, 0]
    q_evac = {}
    u_evac = {}
    o_done = {}
    qu_done = {}
    wring2_slot = [0, 0]

    def stage_qu(h):
        par = h % 2
        first_cast = True
        for wdst, wsrc in ((wqb, wqr_d), (wub, wur_d)):
            for hf in range(4):
                sslot = hf % 2
                p.w("gpsimd", cS, wring2_slot[sslot])
                wdma = p.dma("gpsimd", lambda e, h=h, hf=hf, sslot=sslot, wsrc=wsrc: e.dma_start(out=wring2[sslot][:], in_=wsrc[h, :, hf * 512:(hf + 1) * 512]), dG)
                p.wd("scalar", wdma)
                if first_cast and h >= 2:
                    p.w("scalar", cT, wq_cons[par])
                first_cast = False
                p.e("scalar", lambda e, par=par, hf=hf, sslot=sslot, wdst=wdst: e.copy(out=wdst[par][:, hf * 512:(hf + 1) * 512], in_=wring2[sslot][:]), inc=cS)
                wring2_slot[sslot] = p.v(cS)
        wq_cast[par] = p.v(cS)
        p.w("tensor", cS, wq_cast[par])
        if h == 0:
            p.w("tensor", cV, kv_ready)
        for ng in range(4):
            sl = ng % 2
            if ng >= 2:
                p.w("tensor", cS, q_evac[(h, ng - 2)])
            elif h >= 1:
                p.w("tensor", cS, u_evac[(h - 1, ng + 2)])
            for i in range(NC):
                p.e("tensor", lambda e, h=h, i=i, ng=ng, par=par, sl=sl: e.matmul(qp[sl][:], wqb[par][:, i * 128:(i + 1) * 128], A[:, i * L + ng * 512:i * L + (ng + 1) * 512], start=(i == 0), stop=(i == NC - 1)), inc=(cT if i == NC - 1 else None))
            qmm = p.v(cT)
            p.w("scalar", cT, qmm)
            p.e("scalar", lambda e, h=h, ng=ng, sl=sl, par=par: e.activation(out=qtsb[par][:, ng * 512:(ng + 1) * 512], in_=qp[sl][:], func=AF.Relu, scale=INV_SCALE), inc=cS)
            q_evac[(h, ng)] = p.v(cS)
        for ng in range(4):
            sl = ng % 2
            p.w("tensor", cS, u_evac[(h, ng - 2)] if ng >= 2 else q_evac[(h, 3)])
            for i in range(NC):
                p.e("tensor", lambda e, h=h, i=i, ng=ng, par=par, sl=sl: e.matmul(qp[sl][:], wub[par][:, i * 128:(i + 1) * 128], A[:, i * L + ng * 512:i * L + (ng + 1) * 512], start=(i == 0), stop=(i == NC - 1)), inc=(cT if i == NC - 1 else None))
            umm = p.v(cT)
            p.w("scalar", cT, umm)
            if ng == 0 and h >= 2:
                p.w("scalar", cG, o_done[h - 2])
            p.e("scalar", lambda e, h=h, ng=ng, sl=sl, par=par: e.activation(out=utsb[par][:, ng * 512:(ng + 1) * 512], in_=qp[sl][:], func=AF.Silu), inc=cS)
            u_evac[(h, ng)] = p.v(cS)
        wq_cons[par] = p.v(cT)
        qu_done[h] = p.v(cS)

    def attn_mms(h):
        par = h % 2
        p.w("tensor", cS, qu_done[h])
        for ng in range(4):
            sl = ng % 2
            if ng >= 2:
                p.w("tensor", cV, attn_evac[ng - 2])
            p.e("tensor", lambda e, h=h, ng=ng, sl=sl, par=par: e.matmul(ap_[sl][:], kvsb[:, h * 128:(h + 1) * 128], qtsb[par][:, ng * 512:(ng + 1) * 512], start=True, stop=True), inc=cT)
            amm = p.v(cT)
            p.w("vector", cT, amm)
            p.e("vector", lambda e, ng=ng, sl=sl, par=par: e.tensor_copy(attnsb[par][:, ng * 512:(ng + 1) * 512], ap_[sl][:]), inc=cV)
            attn_evac[ng] = p.v(cV)

    def ssbc(h):
        par = h % 2
        for hh2 in range(2):
          p.e("vector", lambda e, par=par, hh2=hh2: e.tensor_tensor(out=sq32[:], in0=attnsb[par][:, hh2 * 1024:(hh2 + 1) * 1024], in1=attnsb[par][:, hh2 * 1024:(hh2 + 1) * 1024], op=OP.mult), inc=cV)
          sq_done = p.v(cV)
          p.w("scalar", cV, sq_done)
          p.e("scalar", lambda e: e.copy(out=sqh[:], in_=sq32[:]), inc=cS)
          sqh_done = p.v(cS)
          p.w("vector", cS, sqh_done)
          p.e("vector", lambda e: e.tensor_tensor(out=sql[:], in0=sq32[:], in1=sqh[:], op=OP.subtract), inc=cV)
          sql_done = p.v(cV)
          p.w("tensor", cV, sql_done)
          for ng in (2 * hh2, 2 * hh2 + 1):
            sl = ng % 2
            lo = (ng % 2) * 512
            p.e("tensor", lambda e, lo=lo: e.matmul(ssp[0:1, 0:512], onescol[:], sqh[:, lo:lo + 512], start=True, stop=False))
            p.e("tensor", lambda e, lo=lo: e.matmul(ssp[0:1, 0:512], onescol[:], sql[:, lo:lo + 512], start=False, stop=True), inc=cT)
            ssmm = p.v(cT)
            p.w("vector", cT, ssmm)
            p.e("vector", lambda e, ng=ng: e.tensor_scalar(out=srow[0:1, 0:512], in0=ssp[0:1, 0:512], scalar1=1.0 / 128.0, scalar2=1e-24, op0=OP.mult, op1=OP.max), inc=cV)
            ts_done = p.v(cV)
            p.w("scalar", cV, ts_done)
            p.e("scalar", lambda e, ng=ng: e.activation(out=srow2[0:1, 0:512], in_=srow[0:1, 0:512], func=AF.Sqrt), inc=cS)
            sq2 = p.v(cS)
            p.w("vector", cS, sq2)
            p.e("vector", lambda e, ng=ng: e.reciprocal(out=srow[0:1, 0:512], in_=srow2[0:1, 0:512]), inc=cV)
            rec_done = p.v(cV)
            p.w("tensor", cV, rec_done)
            if ng >= 2:
                p.w("tensor", cV, omul1[ng - 2])
            p.e("tensor", lambda e, ng=ng, sl=sl: e.matmul(bcp[sl][:], onesrf[0:1, :], srow[0:1, 0:512], start=True, stop=True), inc=cT)
            bcmm = p.v(cT)
            p.w("vector", cT, bcmm)
            p.e("vector", lambda e, ng=ng, sl=sl, par=par: e.tensor_tensor(out=tmp512[:], in0=attnsb[par][:, ng * 512:(ng + 1) * 512], in1=bcp[sl][:], op=OP.mult), inc=cV)
            omul1[ng] = p.v(cV)
            p.w("gpsimd", cV, omul1[ng])
            p.e("gpsimd", lambda e, h=h, ng=ng, par=par: e.tensor_tensor(out=osb[:, h * L + ng * 512: h * L + (ng + 1) * 512], in0=tmp512[:], in1=utsb[par][:, ng * 512:(ng + 1) * 512], op=OP.mult), inc=cG)
            p.w("vector", cG, p.v(cG))
        o_done[h] = p.v(cG)

    for h in range(NH):
        if h >= 2:
            ssbc(h - 2)
        stage_qu(h)
        if h >= 1:
            attn_mms(h - 1)
    ssbc(NH - 2)
    attn_mms(NH - 1)
    ssbc(NH - 1)
    o_all = p.v(cG)
    p2.close()

    p4 = ExitStack()
    wob = p4.enter_context(nc.sbuf_tensor("wob", [128, NC * 512], BF16))
    woring = [p4.enter_context(nc.sbuf_tensor(f"woring{i}", [128, 512], F32)) for i in range(2)]
    xst = [p4.enter_context(nc.sbuf_tensor(f"xst{i}", [128, 512], F32)) for i in range(2)]
    gst = [p4.enter_context(nc.sbuf_tensor(f"gst{i}", [128, 512], F32)) for i in range(2)]
    sgt = [p4.enter_context(nc.sbuf_tensor(f"sgt{i}", [128, 512], F32)) for i in range(2)]
    ost = [p4.enter_context(nc.sbuf_tensor(f"ost{i}", [128, 512], F32)) for i in range(2)]
    sps = [p4.enter_context(nc.psum_tensor(f"sps{i}", [128, 512], F32)) for i in range(2)]

    wo_slot = [0, 0]
    sg_slot = [0, 0]
    gadd_slot = [0, 0]
    out_slot = [None, None]
    smm_last = 0
    for f in range(4):
        for i in range(NC):
            rs = i % 2
            p.w("gpsimd", cS, wo_slot[rs])
            dd = p.dma("gpsimd", lambda e, i=i, f=f, rs=rs: e.dma_start(out=woring[rs][:], in_=wo_d[i * 128:(i + 1) * 128, f * 512:(f + 1) * 512]), dG)
            p.wd("scalar", dd)
            if f > 0 and i == 0:
                p.w("scalar", cT, smm_last)
            p.e("scalar", lambda e, i=i, rs=rs: e.copy(out=wob[:, i * 512:(i + 1) * 512], in_=woring[rs][:]), inc=cS)
            wo_slot[rs] = p.v(cS)
        wo_ready = p.v(cS)
        p.w("tensor", cS, wo_ready)
        for t in range(NT):
            sl = t % 2
            p.w("gpsimd", cG, gadd_slot[sl])
            p.w("gpsimd", cV, sg_slot[sl])
            if t == 0 and f == 0:
                p.dbar("gpsimd", dG)
            xtok = p.dma("gpsimd", lambda e, t=t, f=f, sl=sl: e.dma_start(out=xst[sl][:], in_=x_d[t * 128:(t + 1) * 128, f * 512:(f + 1) * 512]), dG)
            gtok = p.dma("gpsimd", lambda e, t=t, f=f, sl=sl: e.dma_start(out=gst[sl][:], in_=g_d[t * 128:(t + 1) * 128, f * 512:(f + 1) * 512]), dG)
            p.w("tensor", cV, sg_slot[sl])
            for i in range(NC):
                p.e("tensor", lambda e, i=i, t=t, sl=sl: e.matmul(sps[sl][:], osb[:, i * L + t * 128: i * L + (t + 1) * 128], wob[:, i * 512:(i + 1) * 512], start=(i == 0), stop=(i == NC - 1)), inc=(cT if i == NC - 1 else None))
            smm = p.v(cT)
            smm_last = smm
            p.w("vector", cT, smm)
            p.wd("vector", xtok)
            p.wd("vector", gtok)
            p.w("vector", cG, gadd_slot[sl])
            p.e("vector", lambda e, t=t, sl=sl: e.tensor_tensor(out=sgt[sl][:], in0=sps[sl][:], in1=gst[sl][:], op=OP.mult), inc=cV)
            sg_slot[sl] = p.v(cV)
            p.w("gpsimd", cV, sg_slot[sl])
            p.wd("gpsimd", out_slot[sl])
            p.e("gpsimd", lambda e, t=t, sl=sl: e.tensor_tensor(out=ost[sl][:], in0=sgt[sl][:], in1=xst[sl][:], op=OP.add), inc=cG)
            gadd_slot[sl] = p.v(cG)
            p.w("sync", cG, gadd_slot[sl])
            out_slot[sl] = p.dma("sync", lambda e, t=t, f=f, sl=sl: e.dma_start(out=out_d[t * 128:(t + 1) * 128, f * 512:(f + 1) * 512], in_=ost[sl][:]), dO)
    p4.close()
    p2o.close()

    p.dbar("sync", dO)

    with nc.Block() as blk:
        @blk.tensor
        def _(e):
            for f in p.st["tensor"]:
                f(e)

        @blk.vector
        def _(e):
            for f in p.st["vector"]:
                f(e)

        @blk.scalar
        def _(e):
            for f in p.st["scalar"]:
                f(e)

        @blk.gpsimd
        def _(e):
            for f in p.st["gpsimd"]:
                f(e)

        @blk.sync
        def _(e):
            for f in p.st["sync"]:
                f(e)
    es.close()
    return nc


wring_cast = [0, 0, 0, 0]
attn_evac = [0, 0, 0, 0]
omul1 = [0, 0, 0, 0]


def kernel(L0_attn_x, L0_attn_Wq, L0_attn_Wk, L0_attn_Wv, L0_attn_Wu, L0_attn_Wo,
           L0_ffn_Wr, L0_ffn_br, L0_ffn_W1, L0_ffn_W2):
    X = np.ascontiguousarray(L0_attn_x, dtype=np.float32)
    wq = np.ascontiguousarray(L0_attn_Wq, dtype=np.float32)
    wk = np.ascontiguousarray(L0_attn_Wk, dtype=np.float32)
    wv = np.ascontiguousarray(L0_attn_Wv, dtype=np.float32)
    wu = np.ascontiguousarray(L0_attn_Wu, dtype=np.float32)
    wo = np.ascontiguousarray(L0_attn_Wo, dtype=np.float32)
    wr = np.ascontiguousarray(L0_ffn_Wr, dtype=np.float32)
    br = np.ascontiguousarray(L0_ffn_br, dtype=np.float32)
    w1 = np.ascontiguousarray(L0_ffn_W1, dtype=np.float32)
    w2 = np.ascontiguousarray(L0_ffn_W2, dtype=np.float32)

    wqr = np.ascontiguousarray(wq.reshape(NC, 128, NH, 128).transpose(2, 1, 0, 3).reshape(NH, 128, D))
    wur = np.ascontiguousarray(wu.reshape(NC, 128, NH, 128).transpose(2, 1, 0, 3).reshape(NH, 128, D))
    w1t = np.ascontiguousarray(w1.T)
    w1r = np.ascontiguousarray(w1t.reshape(NC, 128, 64, 128).transpose(2, 1, 0, 3).reshape(64, 128, D))
    w2t = np.ascontiguousarray(w2.T)
    wrt = np.ascontiguousarray(wr.T)
    wrr = np.ascontiguousarray(wrt.reshape(NC, 128, NB).transpose(1, 0, 2).reshape(128, NC * NB))
    brv = np.ascontiguousarray(br.reshape(1, NB))
    idnbf = np.eye(128, dtype=bfloat16)
    onescol = np.ones((128, 1), dtype=bfloat16)
    onesrb = np.ones((1, 128), dtype=bfloat16)
    onesrf = np.ones((1, 128), dtype=np.float32)

    nc = bass.Bass("TRN2", target_bir_lowering=False)
    build(nc)

    shared = dict(wk=wk, wv=wv, wo=wo, wqr=wqr, wur=wur, w1r=w1r, w2t=w2t,
                  wrr=wrr, brd=brv, idnbf=idnbf, onescol=onescol,
                  onesrb=onesrb, onesrf=onesrf,
                  cst=np.tile(np.array([[1e-5, INV_SCALE]], dtype=np.float32), (128, 1)))
    in_maps = [dict(shared, x=np.ascontiguousarray(X[b])) for b in range(8)]
    res = run_bass_kernel_spmd(nc, in_maps, core_ids=list(range(8)))
    return np.stack([np.asarray(r["out"], dtype=np.float32) for r in res.results], axis=0)


# version 09
# speedup vs baseline: 1.0592x; 1.0592x over previous
import math
import numpy as np
from ml_dtypes import bfloat16
from contextlib import ExitStack

import concourse.bass as bass
import concourse.mybir as mybir
from concourse.bass_utils import run_bass_kernel_spmd

F32 = mybir.dt.float32
BF16 = mybir.dt.bfloat16
AF = mybir.ActivationFunctionType
OP = mybir.AluOpType
AX = mybir.AxisListType

D = 2048
L = 2048
NH = 16
DH = 128
NB = 16
BS = 512
JD = 8192
NT = L // 128
NC = D // 128
INV_SCALE = 1.0 / math.sqrt(128.0)

ENGS = ("tensor", "vector", "scalar", "gpsimd", "sync")


class Prog:
    def __init__(self, nc, es):
        self.nc = nc
        self.es = es
        self.st = {k: [] for k in ENGS}
        self.sems = {}
        self.val = {}

    NSL = 4

    def sem(self, name):
        self.sems[name] = self.es.enter_context(self.nc.semaphore(name))
        self.val[name] = 0
        return name

    def dsem(self, name):
        for i in range(self.NSL):
            self.sem(f"{name}{i}")
        self.val[name + "_i"] = 0
        return name

    def dma(self, eng, fn, q):
        i = self.val[q + "_i"] % self.NSL
        self.val[q + "_i"] += 1
        sub = f"{q}{i}"
        self.w(eng, sub, self.val[sub])
        self.val[sub] += 16
        h = self.sems[sub]
        self.st[eng].append(lambda e, fn=fn, h=h: fn(e).then_inc(h, 16))
        return (sub, self.val[sub])

    def wd(self, eng, tok):
        if tok is not None:
            self.w(eng, tok[0], tok[1])

    def dbar(self, eng, q):
        for i in range(self.NSL):
            sub = f"{q}{i}"
            self.w(eng, sub, self.val[sub])

    SELF = {"vector": "cV", "scalar": "cS", "gpsimd": "cG"}

    def w(self, eng, sem, v=None):
        vv = self.val[sem] if v is None else v
        h = self.sems[sem]
        self.st[eng].append(lambda e, h=h, vv=vv: e.wait_ge(h, vv))

    def e(self, eng, fn, inc=None, k=1, dma=False):
        if not dma and eng in self.SELF:
            own = self.SELF[eng]
            self.w(eng, own, self.val[own])
            inc, k = own, 1
        if inc is not None:
            h = self.sems[inc]
            self.val[inc] += k
            self.st[eng].append(lambda e, fn=fn, h=h, k=k: fn(e).then_inc(h, k))
        else:
            self.st[eng].append(lambda e, fn=fn: fn(e))

    def v(self, sem):
        return self.val[sem]


def build(nc):
    es = ExitStack()
    p = Prog(nc, es)
    x_d = nc.dram_tensor("x", [L, D], F32, kind="ExternalInput").ap()
    wk_d = nc.dram_tensor("wk", [D, D], F32, kind="ExternalInput").ap()
    wv_d = nc.dram_tensor("wv", [D, D], F32, kind="ExternalInput").ap()
    wo_d = nc.dram_tensor("wo", [D, D], F32, kind="ExternalInput").ap()
    wqr_d = nc.dram_tensor("wqr", [NH, 128, D], F32, kind="ExternalInput").ap()
    wur_d = nc.dram_tensor("wur", [NH, 128, D], F32, kind="ExternalInput").ap()
    w1r_d = nc.dram_tensor("w1r", [JD // 128, 128, D], F32, kind="ExternalInput").ap()
    w2t_d = nc.dram_tensor("w2t", [JD, D], F32, kind="ExternalInput").ap()
    wrr_d = nc.dram_tensor("wrr", [128, NC * NB], F32, kind="ExternalInput").ap()
    br_d = nc.dram_tensor("brd", [1, NB], F32, kind="ExternalInput").ap()
    cst_d = nc.dram_tensor("cst", [128, 2], F32, kind="ExternalInput").ap()
    idnbf_d = nc.dram_tensor("idnbf", [128, 128], BF16, kind="ExternalInput").ap()
    onescol_d = nc.dram_tensor("onescol", [128, 1], BF16, kind="ExternalInput").ap()
    onesrb_d = nc.dram_tensor("onesrb", [1, 128], BF16, kind="ExternalInput").ap()
    onesrf_d = nc.dram_tensor("onesrf", [1, 128], F32, kind="ExternalInput").ap()
    out_d = nc.dram_tensor("out", [L, D], F32, kind="ExternalOutput").ap()
    g_d = nc.dram_tensor("gscr", [L, D], F32, kind="Internal").ap()

    dS = p.dsem("dS")
    dG = p.dsem("dG")
    dO = p.dsem("dO")
    dV = p.dsem("dV")
    cT = p.sem("cT")
    cV = p.sem("cV")
    cS = p.sem("cS")
    cG = p.sem("cG")

    A = es.enter_context(nc.sbuf_tensor("A", [128, NC * L], BF16))
    kvsb = es.enter_context(nc.sbuf_tensor("kvsb", [128, NH * DH], BF16))
    maskT = es.enter_context(nc.sbuf_tensor("maskT", [128, L], BF16))
    idnbf = es.enter_context(nc.sbuf_tensor("idnbfs", [128, 128], BF16))
    onescol = es.enter_context(nc.sbuf_tensor("onescols", [128, 1], BF16))
    onesrb = es.enter_context(nc.sbuf_tensor("onesrbs", [1, 128], BF16))
    onesrf = es.enter_context(nc.sbuf_tensor("onesrfs", [1, 128], F32))
    wrthi = es.enter_context(nc.sbuf_tensor("wrthi", [128, NC * NB], BF16))
    wrtlo = es.enter_context(nc.sbuf_tensor("wrtlo", [128, NC * NB], BF16))
    brbc = es.enter_context(nc.sbuf_tensor("brbc", [128, NB], F32))
    br1 = es.enter_context(nc.sbuf_tensor("br1", [1, NB], F32))
    cst = es.enter_context(nc.sbuf_tensor("cstsb", [128, 2], F32))
    nc.const_aps.register(F32, 1e-5, cst[:, 0:1])
    nc.const_aps.register(F32, INV_SCALE, cst[:, 1:2])

    for dst, src in ((idnbf, idnbf_d), (onescol, onescol_d),
                     (onesrb, onesrb_d), (onesrf, onesrf_d),
                     (br1, br_d), (cst, cst_d)):
        p.dma("gpsimd", lambda e, dst=dst, src=src: e.dma_start(out=dst[:], in_=src[:]), dG)


    p0 = ExitStack()
    wrt32 = p0.enter_context(nc.sbuf_tensor("wrt32", [128, NC * NB], F32))
    p.dma("gpsimd", lambda e: e.dma_start(out=wrt32[:], in_=wrr_d[:]), dG)
    p.dbar("vector", dG)
    p.e("vector", lambda e: e.tensor_copy(wrthi[:], wrt32[:]), inc=cV)
    p.e("vector", lambda e: e.tensor_tensor(wrtlo[:], wrt32[:], wrthi[:], OP.subtract), inc=cV)
    wrt_ready = p.v(cV)
    xtile = [p0.enter_context(nc.sbuf_tensor(f"xtile{i}", [128, D], F32)) for i in range(2)]
    xn32 = p0.enter_context(nc.sbuf_tensor("xn32", [128, D], F32))
    xnhi = p0.enter_context(nc.sbuf_tensor("xnhi", [128, D], BF16))
    xnlo = p0.enter_context(nc.sbuf_tensor("xnlo", [128, D], BF16))
    sqsc = p0.enter_context(nc.sbuf_tensor("sqsc", [128, D], F32))
    sumx = p0.enter_context(nc.sbuf_tensor("sumx", [128, 1], F32))
    ssq = p0.enter_context(nc.sbuf_tensor("ssq", [128, 1], F32))
    mmean = p0.enter_context(nc.sbuf_tensor("mmean", [128, 1], F32))
    m2 = p0.enter_context(nc.sbuf_tensor("m2", [128, 1], F32))
    varb = p0.enter_context(nc.sbuf_tensor("varb", [128, 1], F32))
    stdb = p0.enter_context(nc.sbuf_tensor("stdb", [128, 1], F32))
    rinv = p0.enter_context(nc.sbuf_tensor("rinv", [128, 1], F32))
    alo = [p0.enter_context(nc.sbuf_tensor(f"alo{i}", [128, 128], BF16)) for i in range(2)]
    logits = p0.enter_context(nc.sbuf_tensor("logits", [128, NB], F32))
    cmp0 = p0.enter_context(nc.sbuf_tensor("cmp0", [128, NB], F32))
    cnt = p0.enter_context(nc.sbuf_tensor("cnt", [128, NB], F32))
    maskb = p0.enter_context(nc.sbuf_tensor("maskb", [128, NB], BF16))
    pt = [p0.enter_context(nc.psum_tensor(f"pt{i}", [128, 128], BF16)) for i in range(2)]
    ptl = [p0.enter_context(nc.psum_tensor(f"ptl{i}", [128, 128], BF16)) for i in range(2)]
    plog = p0.enter_context(nc.psum_tensor("plog", [128, 512], F32))
    pmt = p0.enter_context(nc.psum_tensor("pmt", [128, 128], BF16))

    xdma_done = [None] * NT
    ln_done = [0] * NT
    lnhi_done = [0] * NT
    tr_done = [[0] * NC for _ in range(NT)]
    ahi_done = [[0] * NC for _ in range(NT)]
    alo_done = [[0] * NC for _ in range(NT)]
    rt_done = [0] * NT
    mask_done = [0] * NT

    def p0_ln(t):
        if t >= 2:
            p.w("gpsimd", cV, ln_done[t - 2])
        xdma_done[t] = p.dma("gpsimd", lambda e, t=t: e.dma_start(out=xtile[t % 2][:], in_=x_d[t * 128:(t + 1) * 128, :]), dG)
        p.wd("vector", xdma_done[t])
        p.e("vector", lambda e, t=t: e.reduce_sum(sumx[:], xtile[t % 2][:], axis=AX.X), inc=cV)
        p.wd("scalar", xdma_done[t])
        if t >= 1:
            p.w("scalar", cV, ln_done[t - 1])
        p.e("scalar", lambda e, t=t: e.activation(out=sqsc[:], in_=xtile[t % 2][:], func=AF.Square, accum_out=ssq[:, 0:1]), inc=cS)
        sq_done = p.v(cS)
        p.e("vector", lambda e: e.tensor_scalar(out=mmean[:], in0=sumx[:], scalar1=1.0 / D, scalar2=None, op0=OP.mult))
        p.e("vector", lambda e: e.tensor_tensor(out=m2[:], in0=mmean[:], in1=mmean[:], op=OP.mult))
        p.w("vector", cS, sq_done)
        p.e("vector", lambda e: e.tensor_scalar(out=varb[:], in0=ssq[:], scalar1=1.0 / D, scalar2=m2[:, 0:1], op0=OP.mult, op1=OP.subtract), inc=cV)
        var_done = p.v(cV)
        p.w("scalar", cV, var_done)
        p.e("scalar", lambda e: e.activation(out=stdb[:], in_=varb[:], func=AF.Sqrt, bias=1e-5), inc=cS)
        std_done = p.v(cS)
        p.w("vector", cS, std_done)
        p.e("vector", lambda e: e.reciprocal(out=rinv[:], in_=stdb[:]))
        if t >= 1:
            p.w("vector", cT, tr_done[t - 1][NC - 1])
        p.e("vector", lambda e, t=t: e.tensor_scalar(out=xn32[:], in0=xtile[t % 2][:], scalar1=mmean[:, 0:1], scalar2=rinv[:, 0:1], op0=OP.subtract, op1=OP.mult), inc=cV)
        xn_done = p.v(cV)
        p.w("scalar", cV, xn_done)
        p.e("scalar", lambda e: e.copy(out=xnhi[:], in_=xn32[:]), inc=cS)
        hi_done = p.v(cS)
        p.w("vector", cS, hi_done)
        p.e("vector", lambda e: e.tensor_tensor(out=xnlo[:], in0=xn32[:], in1=xnhi[:], op=OP.subtract), inc=cV)
        ln_done[t] = p.v(cV)
        lnhi_done[t] = hi_done

    def p0_tr(t):
        p.w("tensor", cS, lnhi_done[t])
        p.w("tensor", cV, ln_done[t])
        for c in range(NC):
            slot = pt[c % 2]
            if c >= 2:
                p.w("tensor", cS, ahi_done[t][c - 2])
                p.w("tensor", cV, alo_done[t][c - 2])
            p.e("tensor", lambda e, t=t, c=c, slot=slot: e.transpose(slot[:, 0:128], xnhi[:, c * 128:(c + 1) * 128], idnbf[:]))
            p.e("tensor", lambda e, t=t, c=c, lslot=ptl[c % 2]: e.transpose(lslot[:, 0:128], xnlo[:, c * 128:(c + 1) * 128], idnbf[:]), inc=cT)
            tr_done[t][c] = p.v(cT)
            p.w("scalar", cT, tr_done[t][c])
            p.e("scalar", lambda e, t=t, c=c, slot=slot: e.copy(out=A[:, c * L + t * 128: c * L + (t + 1) * 128], in_=slot[:, 0:128]), inc=cS)
            ahi_done[t][c] = p.v(cS)
            if c >= 2:
                p.w("vector", cT, rt_done_c[c - 2])
            p.w("vector", cT, tr_done[t][c])
            p.e("vector", lambda e, t=t, c=c, lslot=ptl[c % 2]: e.tensor_copy(alo[c % 2][:], lslot[:, 0:128]), inc=cV)
            alo_done[t][c] = p.v(cV)
            p.w("tensor", cS, ahi_done[t][c])
            p.w("tensor", cV, alo_done[t][c])
            st = (c == 0)
            p.e("tensor", lambda e, t=t, c=c, st=st: e.matmul(plog[:, 0:NB], A[:, c * L + t * 128: c * L + (t + 1) * 128], wrthi[:, c * NB:(c + 1) * NB], start=st, stop=False))
            p.e("tensor", lambda e, t=t, c=c: e.matmul(plog[:, 0:NB], A[:, c * L + t * 128: c * L + (t + 1) * 128], wrtlo[:, c * NB:(c + 1) * NB], start=False, stop=False))
            p.e("tensor", lambda e, t=t, c=c: e.matmul(plog[:, 0:NB], alo[c % 2][:], wrthi[:, c * NB:(c + 1) * NB], start=False, stop=(c == NC - 1)), inc=cT)
            rt_done_c[c] = p.v(cT)
        rt_done[t] = p.v(cT)
        p.w("vector", cT, rt_done[t])
        p.w("vector", cS, brbc_done)
        p.e("vector", lambda e: e.tensor_tensor(out=logits[:], in0=plog[:, 0:NB], in1=brbc[:], op=OP.add), inc=cV)
        for i in range(NB):
            p.e("vector", lambda e, i=i: e.tensor_scalar(out=cmp0[:], in0=logits[:], scalar1=logits[:, i:i + 1], scalar2=0.0, op0=OP.is_gt, op1=OP.add, accum_out=cnt[:, i:i + 1]))
        p.e("vector", lambda e: e.tensor_scalar(out=maskb[:], in0=cnt[:], scalar1=3.5, scalar2=None, op0=OP.is_lt), inc=cV)
        mk_done = p.v(cV)
        p.w("tensor", cV, mk_done)
        if t >= 1:
            p.w("tensor", cS, mask_done[t - 1])
        p.e("tensor", lambda e: e.transpose(pmt[0:NB, 0:128], maskb[:], idnbf[:]), inc=cT)
        mt_done = p.v(cT)
        p.w("scalar", cT, mt_done)
        p.e("scalar", lambda e, t=t: e.copy(out=maskT[0:NB, t * 128:(t + 1) * 128], in_=pmt[0:NB, 0:128]), inc=cS)
        mask_done[t] = p.v(cS)

    p.w("tensor", cV, wrt_ready)
    p.dbar("tensor", dG)
    p.e("tensor", lambda e: e.matmul(plog[:, 0:NB], onesrf[0:1, :], br1[0:1, :], start=True, stop=True), inc=cT)
    brmm = p.v(cT)
    p.w("scalar", cT, brmm)
    p.e("scalar", lambda e: e.copy(out=brbc[:], in_=plog[:, 0:NB]), inc=cS)
    brbc_done = p.v(cS)
    p.w("tensor", cS, brbc_done)
    rt_done_c = [0] * NC
    for t in range(NT):
        p0_ln(t)
        p0_tr(t)
    p0_end_T = p.v(cT)
    p0_end_S = p.v(cS)
    p0_end_V = p.v(cV)
    p0.close()
    p.w("sync", cT, p0_end_T)
    p.w("sync", cS, p0_end_S)
    p.w("sync", cV, p0_end_V)

    p1 = ExitStack()
    wring = [p1.enter_context(nc.sbuf_tensor(f"wring{i}", [128, 512], F32)) for i in range(4)]
    wkb = [p1.enter_context(nc.sbuf_tensor(f"wkb{i}", [128, NC * 512], BF16)) for i in range(2)]
    wvb = [p1.enter_context(nc.sbuf_tensor(f"wvb{i}", [128, NC * 512], BF16)) for i in range(2)]
    ksb = [p1.enter_context(nc.sbuf_tensor(f"ksb{i}", [128, 512], BF16)) for i in range(2)]
    vsb = [p1.enter_context(nc.sbuf_tensor(f"vsb{i}", [128, 512], BF16)) for i in range(2)]
    kp = [p1.enter_context(nc.psum_tensor(f"kp{i}", [128, 512], F32)) for i in range(2)]
    vp = [p1.enter_context(nc.psum_tensor(f"vp{i}", [128, 512], F32)) for i in range(2)]
    kvp = p1.enter_context(nc.psum_tensor("kvp", [128, NH * DH], F32))

    wcast_done = [0, 0]
    wcons_done = [0, 0]
    kevac = [0, 0]
    vevac = [0, 0]
    kvmm = [0, 0]

    for ncx in range(4):
        par = ncx % 2
        for i in range(NC):
            rs = (ncx * NC + i) % 4
            p.w("sync", cS, wring_cast[rs])
            dd = p.dma("sync", lambda e, i=i, ncx=ncx, rs=rs: e.dma_start(out=wring[rs][:], in_=wk_d[i * 128:(i + 1) * 128, ncx * 512:(ncx + 1) * 512]), dS)
            p.wd("scalar", dd)
            if ncx >= 2 and i == 0:
                p.w("scalar", cT, wcons_done[par])
            p.e("scalar", lambda e, i=i, par=par, rs=rs: e.copy(out=wkb[par][:, i * 512:(i + 1) * 512], in_=wring[rs][:]), inc=cS)
            wring_cast[rs] = p.v(cS)
        for i in range(NC):
            rs = (ncx * NC + i) % 4
            p.w("sync", cS, wring_cast[rs])
            dd = p.dma("sync", lambda e, i=i, ncx=ncx, rs=rs: e.dma_start(out=wring[rs][:], in_=wv_d[i * 128:(i + 1) * 128, ncx * 512:(ncx + 1) * 512]), dS)
            p.wd("scalar", dd)
            p.e("scalar", lambda e, i=i, par=par, rs=rs: e.copy(out=wvb[par][:, i * 512:(i + 1) * 512], in_=wring[rs][:]), inc=cS)
            wring_cast[rs] = p.v(cS)
        wcast_done[par] = p.v(cS)
        p.w("tensor", cS, wcast_done[par])

        kevac_t = [0] * NT
        vevac_t = [0] * NT
        kv_t = [0] * NT
        for t in range(NT):
            sl = t % 2
            if t >= 2:
                p.w("tensor", cS, kevac_t[t - 2])
            for i in range(NC):
                p.e("tensor", lambda e, i=i, t=t, par=par, sl=sl: e.matmul(kp[sl][:], A[:, i * L + t * 128: i * L + (t + 1) * 128], wkb[par][:, i * 512:(i + 1) * 512], start=(i == 0), stop=(i == NC - 1)), inc=(cT if i == NC - 1 else None))
            kmm = p.v(cT)
            if t >= 2:
                p.w("tensor", cV, vevac_t[t - 2])
            for i in range(NC):
                p.e("tensor", lambda e, i=i, t=t, par=par, sl=sl: e.matmul(vp[sl][:], A[:, i * L + t * 128: i * L + (t + 1) * 128], wvb[par][:, i * 512:(i + 1) * 512], start=(i == 0), stop=(i == NC - 1)), inc=(cT if i == NC - 1 else None))
            vmm = p.v(cT)
            p.w("scalar", cT, kmm)
            if t >= 2:
                p.w("scalar", cT, kv_t[t - 2])
            p.e("scalar", lambda e, sl=sl: e.activation(out=ksb[sl][:], in_=kp[sl][:], func=AF.Relu, scale=INV_SCALE), inc=cS)
            kevac_t[t] = p.v(cS)
            p.w("vector", cT, vmm)
            if t >= 2:
                p.w("vector", cT, kv_t[t - 2])
            p.e("vector", lambda e, sl=sl: e.tensor_copy(vsb[sl][:], vp[sl][:]), inc=cV)
            vevac_t[t] = p.v(cV)
            p.w("tensor", cS, kevac_t[t])
            p.w("tensor", cV, vevac_t[t])
            for hh in range(4):
                h = ncx * 4 + hh
                p.e("tensor", lambda e, h=h, hh=hh, sl=sl, t=t: e.matmul(kvp[:, h * 128:(h + 1) * 128], ksb[sl][:, hh * 128:(hh + 1) * 128], vsb[sl][:, hh * 128:(hh + 1) * 128], start=(t == 0 and hh == 0), stop=(t == NT - 1 and hh == 3)), inc=(cT if hh == 3 else None))
            kv_t[t] = p.v(cT)
        wcons_done[par] = p.v(cT)
        p.w("vector", cT, wcons_done[par])
        p.e("vector", lambda e, ncx=ncx: e.tensor_copy(kvsb[:, ncx * 512:(ncx + 1) * 512], kvp[:, ncx * 512:(ncx + 1) * 512]), inc=cV)
    kv_ready = p.v(cV)
    p1_end_T = p.v(cT)
    p1_end_S = p.v(cS)
    p1.close()
    p.w("sync", cT, p1_end_T)
    p.w("sync", cS, p1_end_S)
    p.w("sync", cV, kv_ready)

    p3 = ExitStack()
    y_acc = p3.enter_context(nc.sbuf_tensor("y_acc", [128, 8 * D], F32))
    hm = p3.enter_context(nc.sbuf_tensor("hm", [128, 8 * 1024], BF16))
    w2b = [p3.enter_context(nc.sbuf_tensor(f"w2b{i}", [128, 8 * 1024], BF16)) for i in range(2)]
    w1ring = [p3.enter_context(nc.sbuf_tensor(f"w1ring{i}", [128, 1024], F32)) for i in range(2)]
    w1b = [p3.enter_context(nc.sbuf_tensor(f"w1b{i}", [128, 1024], BF16)) for i in range(2)]
    w2ring = [p3.enter_context(nc.sbuf_tensor(f"w2ring{i}", [128, 512], F32)) for i in range(2)]
    mbc = [p3.enter_context(nc.sbuf_tensor(f"mbc{i}", [128, 1024], BF16)) for i in range(2)]
    mrow = p3.enter_context(nc.sbuf_tensor("mrow0", [1, 512], BF16))
    hps = [p3.enter_context(nc.psum_tensor(f"hps{i}", [128, 1024], F32)) for i in range(2)]
    yps = [p3.enter_context(nc.psum_tensor(f"yps{i}", [128, 1024], F32)) for i in range(2)]

    w1dma = [0, 0]
    w1cast = [0, 0]
    w1cons = [0, 0]
    gelu_done = {}
    hmmask_done = {}
    gst_dma = [None, None]
    w2cast_slot = [0, 0]
    w2cast_slot_g = [0, 0]
    w2cons_fh = [0, 0]
    yadd_slot = [0, 0]
    mrow_guard = [0, 0]
    silu_mark = 0

    p.w("tensor", cV, kv_ready)
    for Hh in range(2):
        tok0 = Hh * 1024
        for slab in range(8):
            b0 = slab * 2
            if (Hh, slab) != (0, 0):
                p.w("tensor", cS, gelu_done[(Hh, slab - 1, 7) if slab > 0 else (Hh - 1, 7, 7)])
            p.w("tensor", cS, mask_done[NT - 1])
            for bb in range(2):
                b = b0 + bb
                if bb == 1:
                    p.w("tensor", cS, p.v(cS))
                for gg in range(2):
                    p.w("scalar", cS, mask_done[NT - 1])
                    p.w("scalar", cT, mrow_guard[0])
                    mrow_dma = p.dma("scalar", lambda e, b=b, gg=gg, tok0=tok0: e.dma_start(out=mrow[0:1, :], in_=maskT[b:b + 1, tok0 + gg * 512: tok0 + (gg + 1) * 512]), dV)
                    p.wd("tensor", mrow_dma)
                    p.e("tensor", lambda e, gg=gg: e.matmul(hps[0][:, gg * 512:(gg + 1) * 512], onesrb[0:1, :], mrow[0:1, :], start=True, stop=True), inc=cT)
                    mrow_guard[0] = p.v(cT)
                bcmm = p.v(cT)
                p.w("scalar", cT, bcmm)
                p.e("scalar", lambda e, bb=bb: e.copy(out=mbc[bb][:], in_=hps[0][:, 0:1024]), inc=cS)
            mbc_done = p.v(cS)
            p.w("tensor", cS, mbc_done)

            for jc in range(8):
                jg = slab * 8 + jc
                sl = jc % 2
                for hf in range(2):
                    p.w("sync", cS, w1cast[hf])
                    w1dma[hf] = p.dma("sync", lambda e, jg=jg, hf=hf: e.dma_start(out=w1ring[hf][:], in_=w1r_d[jg, :, hf * 1024:(hf + 1) * 1024]), dS)
                    p.wd("scalar", w1dma[hf])
                    p.w("scalar", cT, w1cons[hf])
                    p.e("scalar", lambda e, hf=hf: e.copy(out=w1b[hf][:], in_=w1ring[hf][:]), inc=cS)
                    w1cast[hf] = p.v(cS)
                if (Hh * 64 + jg) >= 1:
                    p.w("tensor", cS, gelu_done.get((Hh, slab, jc - 2), gelu_done.get((Hh, slab - 1, jc + 6), 0)))
                for hf in range(2):
                    p.w("tensor", cS, w1cast[hf])
                    for ng in range(2):
                        for ii in range(8):
                            i = hf * 8 + ii
                            p.e("tensor", lambda e, i=i, ii=ii, ng=ng, sl=sl, hf=hf, tok0=tok0: e.matmul(hps[sl][:, ng * 512:(ng + 1) * 512], w1b[hf][:, ii * 128:(ii + 1) * 128], A[:, i * L + tok0 + ng * 512: i * L + tok0 + (ng + 1) * 512], start=(i == 0), stop=(i == NC - 1)), inc=(cT if (i == NC - 1 and ng == 1) or (ii == 7 and ng == 1) else None))
                    w1cons[hf] = p.v(cT)
                hmm = p.v(cT)
                p.w("scalar", cT, hmm)
                p.e("scalar", lambda e, jc=jc, sl=sl: e.activation(out=hm[:, jc * 1024:(jc + 1) * 1024], in_=hps[sl][:], func=AF.Gelu), inc=cS)
                gelu_done[(Hh, slab, jc)] = p.v(cS)
                p.w("vector", cS, gelu_done[(Hh, slab, jc)])
                p.w("vector", cS, mbc_done)
                p.e("vector", lambda e, jc=jc, bb=jc // 4: e.tensor_tensor(out=hm[:, jc * 1024:(jc + 1) * 1024], in0=hm[:, jc * 1024:(jc + 1) * 1024], in1=mbc[bb][:], op=OP.mult), inc=cV)
                hmmask_done[(Hh, slab, jc)] = p.v(cV)
            hm_ready = p.v(cV)

            for fh in range(2):
                qeng = "sync" if fh == 0 else "gpsimd"
                qsem = dS if fh == 0 else dG
                for jc in range(8):
                    jg = slab * 8 + jc
                    for cc in range(2):
                        p.w(qeng, cS, w2cast_slot[cc])
                        dd = p.dma(qeng, lambda e, jg=jg, fh=fh, cc=cc: e.dma_start(out=w2ring[cc][:], in_=w2t_d[jg * 128:(jg + 1) * 128, fh * 1024 + cc * 512: fh * 1024 + (cc + 1) * 512]), qsem)
                        p.wd("scalar", dd)
                        if jc == 0 and cc == 0:
                            p.w("scalar", cT, w2cons_fh[fh])
                        p.e("scalar", lambda e, jc=jc, fh=fh, cc=cc: e.copy(out=w2b[fh][:, jc * 1024 + cc * 512: jc * 1024 + (cc + 1) * 512], in_=w2ring[cc][:]), inc=cS)
                        w2cast_slot[cc] = p.v(cS)
                w2_ready = p.v(cS)
                p.w("tensor", cV, hm_ready)
                p.w("tensor", cS, w2_ready)
                for t8 in range(8):
                    sl = t8 % 2
                    p.w("tensor", cV, yadd_slot[sl])
                    for jc in range(8):
                        for fg in range(2):
                            p.e("tensor", lambda e, t8=t8, jc=jc, fg=fg, fh=fh, sl=sl: e.matmul(yps[sl][:, fg * 512:(fg + 1) * 512], hm[:, jc * 1024 + t8 * 128: jc * 1024 + (t8 + 1) * 128], w2b[fh][:, jc * 1024 + fg * 512: jc * 1024 + (fg + 1) * 512], start=(jc == 0), stop=(jc == 7)), inc=(cT if (jc == 7 and fg == 1) else None))
                    ymm = p.v(cT)
                    p.w("vector", cT, ymm)
                    if slab == 0:
                        if Hh == 1:
                            p.w("vector", cS, silu_mark)
                            p.dbar("vector", dG)
                        p.e("vector", lambda e, t8=t8, fh=fh, sl=sl: e.tensor_copy(y_acc[:, t8 * D + fh * 1024: t8 * D + (fh + 1) * 1024], yps[sl][:]), inc=cV)
                    else:
                        p.e("vector", lambda e, t8=t8, fh=fh, sl=sl: e.tensor_tensor(out=y_acc[:, t8 * D + fh * 1024: t8 * D + (fh + 1) * 1024], in0=y_acc[:, t8 * D + fh * 1024: t8 * D + (fh + 1) * 1024], in1=yps[sl][:], op=OP.add), inc=cV)
                    yadd_slot[sl] = p.v(cV)
                w2cons_fh[fh] = p.v(cT)
        yfin = p.v(cV)
        for t8 in range(8):
            p.w("scalar", cV, yfin)
            p.e("scalar", lambda e, t8=t8: e.activation(out=y_acc[:, t8 * D:(t8 + 1) * D], in_=y_acc[:, t8 * D:(t8 + 1) * D], func=AF.Silu), inc=cS)
            gdone = p.v(cS)
            p.w("gpsimd", cS, gdone)
            gst_dma[0] = p.dma("gpsimd", lambda e, t8=t8, tok0=tok0: e.dma_start(out=g_d[tok0 + t8 * 128: tok0 + (t8 + 1) * 128, :], in_=y_acc[:, t8 * D:(t8 + 1) * D]), dG)
        silu_mark = p.v(cS)
    p3.close()

    p2o = ExitStack()
    osb = p2o.enter_context(nc.sbuf_tensor("osb", [128, NH * L], BF16))
    p2 = ExitStack()
    wring2 = [p2.enter_context(nc.sbuf_tensor(f"wring2{i}", [128, 512], F32)) for i in range(2)]
    wqb = [p2.enter_context(nc.sbuf_tensor(f"wqb{i}", [128, D], BF16)) for i in range(2)]
    wub = [p2.enter_context(nc.sbuf_tensor(f"wub{i}", [128, D], BF16)) for i in range(2)]
    qtsb = [p2.enter_context(nc.sbuf_tensor(f"qtsb{i}", [128, L], BF16)) for i in range(2)]
    utsb = [p2.enter_context(nc.sbuf_tensor(f"utsb{i}", [128, L], BF16)) for i in range(2)]
    attnsb = [p2.enter_context(nc.sbuf_tensor(f"attnsb{i}", [128, L], F32)) for i in range(2)]
    sq32 = p2.enter_context(nc.sbuf_tensor("sq32", [128, 1024], F32))
    sqh = p2.enter_context(nc.sbuf_tensor("sqh", [128, 1024], BF16))
    sql = p2.enter_context(nc.sbuf_tensor("sql", [128, 1024], BF16))
    srow = p2.enter_context(nc.sbuf_tensor("srow", [1, 512], F32))
    srow2 = p2.enter_context(nc.sbuf_tensor("srow2", [1, 512], F32))
    tmp512 = p2.enter_context(nc.sbuf_tensor("tmp512", [128, 512], F32))
    qp = [p2.enter_context(nc.psum_tensor(f"qp{i}", [128, 512], F32)) for i in range(2)]
    ap_ = [p2.enter_context(nc.psum_tensor(f"ap{i}", [128, 512], F32)) for i in range(2)]
    ssp = p2.enter_context(nc.psum_tensor("ssp", [128, 512], F32))
    bcp = [p2.enter_context(nc.psum_tensor(f"bcp{i}", [128, 512], F32)) for i in range(2)]

    wq_cast = [0, 0]
    wq_cons = [0, 0]
    q_evac = {}
    u_evac = {}
    o_done = {}
    au_done = {}
    qu_done = {}
    wring2_slot = [0, 0]

    def stage_qu(h):
        par = h % 2
        first_cast = True
        for wdst, wsrc in ((wqb, wqr_d), (wub, wur_d)):
            for hf in range(4):
                sslot = hf % 2
                p.w("gpsimd", cS, wring2_slot[sslot])
                wdma = p.dma("gpsimd", lambda e, h=h, hf=hf, sslot=sslot, wsrc=wsrc: e.dma_start(out=wring2[sslot][:], in_=wsrc[h, :, hf * 512:(hf + 1) * 512]), dG)
                p.wd("scalar", wdma)
                if first_cast and h >= 2:
                    p.w("scalar", cT, wq_cons[par])
                first_cast = False
                p.e("scalar", lambda e, par=par, hf=hf, sslot=sslot, wdst=wdst: e.copy(out=wdst[par][:, hf * 512:(hf + 1) * 512], in_=wring2[sslot][:]), inc=cS)
                wring2_slot[sslot] = p.v(cS)
        wq_cast[par] = p.v(cS)
        p.w("tensor", cS, wq_cast[par])
        if h == 0:
            p.w("tensor", cV, kv_ready)
        for ng in range(4):
            sl = ng % 2
            if ng >= 2:
                p.w("tensor", cS, q_evac[(h, ng - 2)])
            elif h >= 1:
                p.w("tensor", cS, u_evac[(h - 1, ng + 2)])
            for i in range(NC):
                p.e("tensor", lambda e, h=h, i=i, ng=ng, par=par, sl=sl: e.matmul(qp[sl][:], wqb[par][:, i * 128:(i + 1) * 128], A[:, i * L + ng * 512:i * L + (ng + 1) * 512], start=(i == 0), stop=(i == NC - 1)), inc=(cT if i == NC - 1 else None))
            qmm = p.v(cT)
            p.w("scalar", cT, qmm)
            p.e("scalar", lambda e, h=h, ng=ng, sl=sl, par=par: e.activation(out=qtsb[par][:, ng * 512:(ng + 1) * 512], in_=qp[sl][:], func=AF.Relu, scale=INV_SCALE), inc=cS)
            q_evac[(h, ng)] = p.v(cS)
        for ng in range(4):
            sl = ng % 2
            p.w("tensor", cS, u_evac[(h, ng - 2)] if ng >= 2 else q_evac[(h, 3)])
            for i in range(NC):
                p.e("tensor", lambda e, h=h, i=i, ng=ng, par=par, sl=sl: e.matmul(qp[sl][:], wub[par][:, i * 128:(i + 1) * 128], A[:, i * L + ng * 512:i * L + (ng + 1) * 512], start=(i == 0), stop=(i == NC - 1)), inc=(cT if i == NC - 1 else None))
            umm = p.v(cT)
            p.w("scalar", cT, umm)
            if ng == 0 and h >= 2:
                p.w("scalar", cV, au_done[h - 2])
            p.e("scalar", lambda e, h=h, ng=ng, sl=sl, par=par: e.activation(out=utsb[par][:, ng * 512:(ng + 1) * 512], in_=qp[sl][:], func=AF.Silu), inc=cS)
            u_evac[(h, ng)] = p.v(cS)
        wq_cons[par] = p.v(cT)
        qu_done[h] = p.v(cS)

    def attn_mms(h):
        par = h % 2
        p.w("tensor", cS, qu_done[h])
        au_done[h] = None
        for ng in range(4):
            sl = ng % 2
            if ng >= 2:
                p.w("tensor", cV, attn_evac[ng - 2])
            p.e("tensor", lambda e, h=h, ng=ng, sl=sl, par=par: e.matmul(ap_[sl][:], kvsb[:, h * 128:(h + 1) * 128], qtsb[par][:, ng * 512:(ng + 1) * 512], start=True, stop=True), inc=cT)
            amm = p.v(cT)
            p.w("vector", cT, amm)
            p.e("vector", lambda e, ng=ng, sl=sl, par=par: e.tensor_copy(attnsb[par][:, ng * 512:(ng + 1) * 512], ap_[sl][:]), inc=cV)
            attn_evac[ng] = p.v(cV)
            p.e("vector", lambda e, h=h, ng=ng, par=par: e.tensor_tensor(out=osb[:, h * L + ng * 512: h * L + (ng + 1) * 512], in0=attnsb[par][:, ng * 512:(ng + 1) * 512], in1=utsb[par][:, ng * 512:(ng + 1) * 512], op=OP.mult), inc=cV)
        au_done[h] = p.v(cV)

    def ssbc(h):
        par = h % 2
        for hh2 in range(2):
          p.e("vector", lambda e, par=par, hh2=hh2: e.tensor_tensor(out=sq32[:], in0=attnsb[par][:, hh2 * 1024:(hh2 + 1) * 1024], in1=attnsb[par][:, hh2 * 1024:(hh2 + 1) * 1024], op=OP.mult), inc=cV)
          sq_done = p.v(cV)
          p.w("scalar", cV, sq_done)
          p.e("scalar", lambda e: e.copy(out=sqh[:], in_=sq32[:]), inc=cS)
          sqh_done = p.v(cS)
          p.w("vector", cS, sqh_done)
          p.e("vector", lambda e: e.tensor_tensor(out=sql[:], in0=sq32[:], in1=sqh[:], op=OP.subtract), inc=cV)
          sql_done = p.v(cV)
          p.w("tensor", cV, sql_done)
          for ng in (2 * hh2, 2 * hh2 + 1):
            sl = ng % 2
            lo = (ng % 2) * 512
            p.e("tensor", lambda e, lo=lo: e.matmul(ssp[0:1, 0:512], onescol[:], sqh[:, lo:lo + 512], start=True, stop=False))
            p.e("tensor", lambda e, lo=lo: e.matmul(ssp[0:1, 0:512], onescol[:], sql[:, lo:lo + 512], start=False, stop=True), inc=cT)
            ssmm = p.v(cT)
            p.w("vector", cT, ssmm)
            p.e("vector", lambda e, ng=ng: e.tensor_scalar(out=srow[0:1, 0:512], in0=ssp[0:1, 0:512], scalar1=1.0 / 128.0, scalar2=1e-24, op0=OP.mult, op1=OP.max), inc=cV)
            ts_done = p.v(cV)
            p.w("scalar", cV, ts_done)
            p.e("scalar", lambda e: e.activation(out=srow2[0:1, 0:512], in_=srow[0:1, 0:512], func=AF.Ln), inc=cS)
            p.e("scalar", lambda e: e.activation(out=srow[0:1, 0:512], in_=srow2[0:1, 0:512], func=AF.Exp, scale=-0.5), inc=cS)
            rec_done = p.v(cS)
            p.w("tensor", cS, rec_done)
            if ng >= 2:
                p.w("tensor", cV, omul1[ng - 2])
            p.e("tensor", lambda e, ng=ng, sl=sl: e.matmul(bcp[sl][:], onesrf[0:1, :], srow[0:1, 0:512], start=True, stop=True), inc=cT)
            bcmm = p.v(cT)
            p.w("vector", cT, bcmm)
            p.e("vector", lambda e, h=h, ng=ng, sl=sl: e.tensor_tensor(out=osb[:, h * L + ng * 512: h * L + (ng + 1) * 512], in0=osb[:, h * L + ng * 512: h * L + (ng + 1) * 512], in1=bcp[sl][:], op=OP.mult), inc=cV)
            omul1[ng] = p.v(cV)
        o_done[h] = p.v(cV)

    for h in range(NH):
        stage_qu(h)
        if h >= 1:
            attn_mms(h - 1)
        if h >= 2:
            ssbc(h - 2)
    attn_mms(NH - 1)
    ssbc(NH - 2)
    ssbc(NH - 1)
    p2_end_V = p.v(cV)
    p2_end_S = p.v(cS)
    p2_end_T = p.v(cT)
    p2.close()

    p4 = ExitStack()
    wob = p4.enter_context(nc.sbuf_tensor("wob", [128, NC * 512], BF16))
    woring = [p4.enter_context(nc.sbuf_tensor(f"woring{i}", [128, 512], F32)) for i in range(2)]
    xst = [p4.enter_context(nc.sbuf_tensor(f"xst{i}", [128, 512], F32)) for i in range(2)]
    gst = [p4.enter_context(nc.sbuf_tensor(f"gst{i}", [128, 512], F32)) for i in range(2)]
    sgt = [p4.enter_context(nc.sbuf_tensor(f"sgt{i}", [128, 512], F32)) for i in range(2)]
    ost = [p4.enter_context(nc.sbuf_tensor(f"ost{i}", [128, 512], F32)) for i in range(2)]
    sps = [p4.enter_context(nc.psum_tensor(f"sps{i}", [128, 512], F32)) for i in range(2)]

    wo_slot = [0, 0]
    sg_slot = [0, 0]
    gadd_slot = [0, 0]
    out_slot = [None, None]
    smm_last = 0
    p.w("gpsimd", cV, p2_end_V)
    p.w("gpsimd", cS, p2_end_S)
    p.w("gpsimd", cT, p2_end_T)
    for f in range(4):
        for i in range(NC):
            rs = i % 2
            p.w("gpsimd", cS, wo_slot[rs])
            dd = p.dma("gpsimd", lambda e, i=i, f=f, rs=rs: e.dma_start(out=woring[rs][:], in_=wo_d[i * 128:(i + 1) * 128, f * 512:(f + 1) * 512]), dG)
            p.wd("scalar", dd)
            if f > 0 and i == 0:
                p.w("scalar", cT, smm_last)
            p.e("scalar", lambda e, i=i, rs=rs: e.copy(out=wob[:, i * 512:(i + 1) * 512], in_=woring[rs][:]), inc=cS)
            wo_slot[rs] = p.v(cS)
        wo_ready = p.v(cS)
        p.w("tensor", cS, wo_ready)
        for t in range(NT):
            sl = t % 2
            p.w("gpsimd", cG, gadd_slot[sl])
            p.w("gpsimd", cV, sg_slot[sl])
            if t == 0 and f == 0:
                p.dbar("gpsimd", dG)
            xtok = p.dma("gpsimd", lambda e, t=t, f=f, sl=sl: e.dma_start(out=xst[sl][:], in_=x_d[t * 128:(t + 1) * 128, f * 512:(f + 1) * 512]), dG)
            gtok = p.dma("gpsimd", lambda e, t=t, f=f, sl=sl: e.dma_start(out=gst[sl][:], in_=g_d[t * 128:(t + 1) * 128, f * 512:(f + 1) * 512]), dG)
            p.w("tensor", cV, sg_slot[sl])
            for i in range(NC):
                p.e("tensor", lambda e, i=i, t=t, sl=sl: e.matmul(sps[sl][:], osb[:, i * L + t * 128: i * L + (t + 1) * 128], wob[:, i * 512:(i + 1) * 512], start=(i == 0), stop=(i == NC - 1)), inc=(cT if i == NC - 1 else None))
            smm = p.v(cT)
            smm_last = smm
            p.w("vector", cT, smm)
            p.wd("vector", xtok)
            p.wd("vector", gtok)
            p.w("vector", cG, gadd_slot[sl])
            p.e("vector", lambda e, t=t, sl=sl: e.tensor_tensor(out=sgt[sl][:], in0=sps[sl][:], in1=gst[sl][:], op=OP.mult), inc=cV)
            sg_slot[sl] = p.v(cV)
            p.w("gpsimd", cV, sg_slot[sl])
            p.wd("gpsimd", out_slot[sl])
            p.e("gpsimd", lambda e, t=t, sl=sl: e.tensor_tensor(out=ost[sl][:], in0=sgt[sl][:], in1=xst[sl][:], op=OP.add), inc=cG)
            gadd_slot[sl] = p.v(cG)
            p.w("sync", cG, gadd_slot[sl])
            out_slot[sl] = p.dma("sync", lambda e, t=t, f=f, sl=sl: e.dma_start(out=out_d[t * 128:(t + 1) * 128, f * 512:(f + 1) * 512], in_=ost[sl][:]), dO)
    p4.close()
    p2o.close()

    p.dbar("sync", dO)

    with nc.Block() as blk:
        @blk.tensor
        def _(e):
            for f in p.st["tensor"]:
                f(e)

        @blk.vector
        def _(e):
            for f in p.st["vector"]:
                f(e)

        @blk.scalar
        def _(e):
            for f in p.st["scalar"]:
                f(e)

        @blk.gpsimd
        def _(e):
            for f in p.st["gpsimd"]:
                f(e)

        @blk.sync
        def _(e):
            for f in p.st["sync"]:
                f(e)
    es.close()
    return nc


wring_cast = [0, 0, 0, 0]
attn_evac = [0, 0, 0, 0]
omul1 = [0, 0, 0, 0]


def kernel(L0_attn_x, L0_attn_Wq, L0_attn_Wk, L0_attn_Wv, L0_attn_Wu, L0_attn_Wo,
           L0_ffn_Wr, L0_ffn_br, L0_ffn_W1, L0_ffn_W2):
    X = np.ascontiguousarray(L0_attn_x, dtype=np.float32)
    wq = np.ascontiguousarray(L0_attn_Wq, dtype=np.float32)
    wk = np.ascontiguousarray(L0_attn_Wk, dtype=np.float32)
    wv = np.ascontiguousarray(L0_attn_Wv, dtype=np.float32)
    wu = np.ascontiguousarray(L0_attn_Wu, dtype=np.float32)
    wo = np.ascontiguousarray(L0_attn_Wo, dtype=np.float32)
    wr = np.ascontiguousarray(L0_ffn_Wr, dtype=np.float32)
    br = np.ascontiguousarray(L0_ffn_br, dtype=np.float32)
    w1 = np.ascontiguousarray(L0_ffn_W1, dtype=np.float32)
    w2 = np.ascontiguousarray(L0_ffn_W2, dtype=np.float32)

    wqr = np.ascontiguousarray(wq.reshape(NC, 128, NH, 128).transpose(2, 1, 0, 3).reshape(NH, 128, D))
    wur = np.ascontiguousarray(wu.reshape(NC, 128, NH, 128).transpose(2, 1, 0, 3).reshape(NH, 128, D))
    w1t = np.ascontiguousarray(w1.T)
    w1r = np.ascontiguousarray(w1t.reshape(NC, 128, 64, 128).transpose(2, 1, 0, 3).reshape(64, 128, D))
    w2t = np.ascontiguousarray(w2.T)
    wrt = np.ascontiguousarray(wr.T)
    wrr = np.ascontiguousarray(wrt.reshape(NC, 128, NB).transpose(1, 0, 2).reshape(128, NC * NB))
    brv = np.ascontiguousarray(br.reshape(1, NB))
    idnbf = np.eye(128, dtype=bfloat16)
    onescol = np.ones((128, 1), dtype=bfloat16)
    onesrb = np.ones((1, 128), dtype=bfloat16)
    onesrf = np.ones((1, 128), dtype=np.float32)

    nc = bass.Bass("TRN2", target_bir_lowering=False)
    build(nc)

    shared = dict(wk=wk, wv=wv, wo=wo, wqr=wqr, wur=wur, w1r=w1r, w2t=w2t,
                  wrr=wrr, brd=brv, idnbf=idnbf, onescol=onescol,
                  onesrb=onesrb, onesrf=onesrf,
                  cst=np.tile(np.array([[1e-5, INV_SCALE]], dtype=np.float32), (128, 1)))
    in_maps = [dict(shared, x=np.ascontiguousarray(X[b])) for b in range(8)]
    res = run_bass_kernel_spmd(nc, in_maps, core_ids=list(range(8)))
    return np.stack([np.asarray(r["out"], dtype=np.float32) for r in res.results], axis=0)


# version 10 (final)
# speedup vs baseline: 1.0659x; 1.0064x over previous
import math
import numpy as np
from ml_dtypes import bfloat16
from contextlib import ExitStack

import concourse.bass as bass
import concourse.mybir as mybir
from concourse.bass_utils import run_bass_kernel_spmd

F32 = mybir.dt.float32
BF16 = mybir.dt.bfloat16
AF = mybir.ActivationFunctionType
OP = mybir.AluOpType
AX = mybir.AxisListType

D = 2048
L = 2048
NH = 16
DH = 128
NB = 16
BS = 512
JD = 8192
NT = L // 128
NC = D // 128
INV_SCALE = 1.0 / math.sqrt(128.0)

ENGS = ("tensor", "vector", "scalar", "gpsimd", "sync")


class Prog:
    def __init__(self, nc, es):
        self.nc = nc
        self.es = es
        self.st = {k: [] for k in ENGS}
        self.sems = {}
        self.val = {}

    NSL = 4

    def sem(self, name):
        self.sems[name] = self.es.enter_context(self.nc.semaphore(name))
        self.val[name] = 0
        return name

    def dsem(self, name):
        for i in range(self.NSL):
            self.sem(f"{name}{i}")
        self.val[name + "_i"] = 0
        return name

    def dma(self, eng, fn, q):
        i = self.val[q + "_i"] % self.NSL
        self.val[q + "_i"] += 1
        sub = f"{q}{i}"
        self.w(eng, sub, self.val[sub])
        self.val[sub] += 16
        h = self.sems[sub]
        self.st[eng].append(lambda e, fn=fn, h=h: fn(e).then_inc(h, 16))
        return (sub, self.val[sub])

    def wd(self, eng, tok):
        if tok is not None:
            self.w(eng, tok[0], tok[1])

    def dbar(self, eng, q):
        for i in range(self.NSL):
            sub = f"{q}{i}"
            self.w(eng, sub, self.val[sub])

    SELF = {"vector": "cV", "scalar": "cS", "gpsimd": "cG"}

    def w(self, eng, sem, v=None):
        vv = self.val[sem] if v is None else v
        h = self.sems[sem]
        self.st[eng].append(lambda e, h=h, vv=vv: e.wait_ge(h, vv))

    def e(self, eng, fn, inc=None, k=1, dma=False):
        if not dma and eng in self.SELF:
            own = self.SELF[eng]
            self.w(eng, own, self.val[own])
            inc, k = own, 1
        if inc is not None:
            h = self.sems[inc]
            self.val[inc] += k
            self.st[eng].append(lambda e, fn=fn, h=h, k=k: fn(e).then_inc(h, k))
        else:
            self.st[eng].append(lambda e, fn=fn: fn(e))

    def v(self, sem):
        return self.val[sem]


def build(nc):
    es = ExitStack()
    p = Prog(nc, es)
    x_d = nc.dram_tensor("x", [L, D], F32, kind="ExternalInput").ap()
    wk_d = nc.dram_tensor("wk", [D, D], F32, kind="ExternalInput").ap()
    wv_d = nc.dram_tensor("wv", [D, D], F32, kind="ExternalInput").ap()
    wo_d = nc.dram_tensor("wo", [D, D], F32, kind="ExternalInput").ap()
    wqr_d = nc.dram_tensor("wqr", [NH, 128, D], F32, kind="ExternalInput").ap()
    wur_d = nc.dram_tensor("wur", [NH, 128, D], F32, kind="ExternalInput").ap()
    w1r_d = nc.dram_tensor("w1r", [JD // 128, 128, D], F32, kind="ExternalInput").ap()
    w2t_d = nc.dram_tensor("w2t", [JD, D], F32, kind="ExternalInput").ap()
    wrr_d = nc.dram_tensor("wrr", [128, NC * NB], F32, kind="ExternalInput").ap()
    br_d = nc.dram_tensor("brd", [1, NB], F32, kind="ExternalInput").ap()
    cst_d = nc.dram_tensor("cst", [128, 2], F32, kind="ExternalInput").ap()
    idnbf_d = nc.dram_tensor("idnbf", [128, 128], BF16, kind="ExternalInput").ap()
    onescol_d = nc.dram_tensor("onescol", [128, 1], BF16, kind="ExternalInput").ap()
    onesrb_d = nc.dram_tensor("onesrb", [1, 128], BF16, kind="ExternalInput").ap()
    onesrf_d = nc.dram_tensor("onesrf", [1, 128], F32, kind="ExternalInput").ap()
    out_d = nc.dram_tensor("out", [L, D], F32, kind="ExternalOutput").ap()
    g_d = nc.dram_tensor("gscr", [L, D], F32, kind="Internal").ap()

    dS = p.dsem("dS")
    dG = p.dsem("dG")
    dO = p.dsem("dO")
    dV = p.dsem("dV")
    cT = p.sem("cT")
    cV = p.sem("cV")
    cS = p.sem("cS")
    cG = p.sem("cG")

    A = es.enter_context(nc.sbuf_tensor("A", [128, NC * L], BF16))
    kvsb = es.enter_context(nc.sbuf_tensor("kvsb", [128, NH * DH], BF16))
    maskT = es.enter_context(nc.sbuf_tensor("maskT", [128, L], BF16))
    idnbf = es.enter_context(nc.sbuf_tensor("idnbfs", [128, 128], BF16))
    onescol = es.enter_context(nc.sbuf_tensor("onescols", [128, 1], BF16))
    onesrb = es.enter_context(nc.sbuf_tensor("onesrbs", [1, 128], BF16))
    onesrf = es.enter_context(nc.sbuf_tensor("onesrfs", [1, 128], F32))
    wrthi = es.enter_context(nc.sbuf_tensor("wrthi", [128, NC * NB], BF16))
    wrtlo = es.enter_context(nc.sbuf_tensor("wrtlo", [128, NC * NB], BF16))
    brbc = es.enter_context(nc.sbuf_tensor("brbc", [128, NB], F32))
    br1 = es.enter_context(nc.sbuf_tensor("br1", [1, NB], F32))
    cst = es.enter_context(nc.sbuf_tensor("cstsb", [128, 2], F32))
    nc.const_aps.register(F32, 1e-5, cst[:, 0:1])
    nc.const_aps.register(F32, INV_SCALE, cst[:, 1:2])

    for dst, src in ((idnbf, idnbf_d), (onescol, onescol_d),
                     (onesrb, onesrb_d), (onesrf, onesrf_d),
                     (br1, br_d), (cst, cst_d)):
        p.dma("gpsimd", lambda e, dst=dst, src=src: e.dma_start(out=dst[:], in_=src[:]), dG)


    p0 = ExitStack()
    wrt32 = p0.enter_context(nc.sbuf_tensor("wrt32", [128, NC * NB], F32))
    p.dma("gpsimd", lambda e: e.dma_start(out=wrt32[:], in_=wrr_d[:]), dG)
    p.dbar("vector", dG)
    p.e("vector", lambda e: e.tensor_copy(wrthi[:], wrt32[:]), inc=cV)
    p.e("vector", lambda e: e.tensor_tensor(wrtlo[:], wrt32[:], wrthi[:], OP.subtract), inc=cV)
    wrt_ready = p.v(cV)
    xtile = [p0.enter_context(nc.sbuf_tensor(f"xtile{i}", [128, D], F32)) for i in range(2)]
    xn32 = p0.enter_context(nc.sbuf_tensor("xn32", [128, D], F32))
    xnhi = p0.enter_context(nc.sbuf_tensor("xnhi", [128, D], BF16))
    xnlo = p0.enter_context(nc.sbuf_tensor("xnlo", [128, D], BF16))
    sqsc = p0.enter_context(nc.sbuf_tensor("sqsc", [128, D], F32))
    sumx = p0.enter_context(nc.sbuf_tensor("sumx", [128, 1], F32))
    ssq = p0.enter_context(nc.sbuf_tensor("ssq", [128, 1], F32))
    mmean = p0.enter_context(nc.sbuf_tensor("mmean", [128, 1], F32))
    m2 = p0.enter_context(nc.sbuf_tensor("m2", [128, 1], F32))
    varb = p0.enter_context(nc.sbuf_tensor("varb", [128, 1], F32))
    stdb = p0.enter_context(nc.sbuf_tensor("stdb", [128, 1], F32))
    rinv = p0.enter_context(nc.sbuf_tensor("rinv", [128, 1], F32))
    alo = [p0.enter_context(nc.sbuf_tensor(f"alo{i}", [128, 128], BF16)) for i in range(2)]
    logits = p0.enter_context(nc.sbuf_tensor("logits", [128, NB], F32))
    cmp0 = p0.enter_context(nc.sbuf_tensor("cmp0", [128, NB], F32))
    cnt = p0.enter_context(nc.sbuf_tensor("cnt", [128, NB], F32))
    maskb = p0.enter_context(nc.sbuf_tensor("maskb", [128, NB], BF16))
    pt = [p0.enter_context(nc.psum_tensor(f"pt{i}", [128, 128], BF16)) for i in range(2)]
    ptl = [p0.enter_context(nc.psum_tensor(f"ptl{i}", [128, 128], BF16)) for i in range(2)]
    plog = p0.enter_context(nc.psum_tensor("plog", [128, 512], F32))
    pmt = p0.enter_context(nc.psum_tensor("pmt", [128, 128], BF16))

    xdma_done = [None] * NT
    ln_done = [0] * NT
    lnhi_done = [0] * NT
    tr_done = [[0] * NC for _ in range(NT)]
    ahi_done = [[0] * NC for _ in range(NT)]
    alo_done = [[0] * NC for _ in range(NT)]
    rt_done = [0] * NT
    mask_done = [0] * NT

    def p0_ln(t):
        if t >= 2:
            p.w("gpsimd", cV, ln_done[t - 2])
        xdma_done[t] = p.dma("gpsimd", lambda e, t=t: e.dma_start(out=xtile[t % 2][:], in_=x_d[t * 128:(t + 1) * 128, :]), dG)
        p.wd("vector", xdma_done[t])
        p.e("vector", lambda e, t=t: e.reduce_sum(sumx[:], xtile[t % 2][:], axis=AX.X), inc=cV)
        p.wd("scalar", xdma_done[t])
        if t >= 1:
            p.w("scalar", cV, ln_done[t - 1])
        p.e("scalar", lambda e, t=t: e.activation(out=sqsc[:], in_=xtile[t % 2][:], func=AF.Square, accum_out=ssq[:, 0:1]), inc=cS)
        sq_done = p.v(cS)
        p.e("vector", lambda e: e.tensor_scalar(out=mmean[:], in0=sumx[:], scalar1=1.0 / D, scalar2=None, op0=OP.mult))
        p.e("vector", lambda e: e.tensor_tensor(out=m2[:], in0=mmean[:], in1=mmean[:], op=OP.mult))
        p.w("vector", cS, sq_done)
        p.e("vector", lambda e: e.tensor_scalar(out=varb[:], in0=ssq[:], scalar1=1.0 / D, scalar2=m2[:, 0:1], op0=OP.mult, op1=OP.subtract), inc=cV)
        var_done = p.v(cV)
        p.w("scalar", cV, var_done)
        p.e("scalar", lambda e: e.activation(out=stdb[:], in_=varb[:], func=AF.Sqrt, bias=1e-5), inc=cS)
        std_done = p.v(cS)
        p.w("vector", cS, std_done)
        p.e("vector", lambda e: e.reciprocal(out=rinv[:], in_=stdb[:]))
        if t >= 1:
            p.w("vector", cT, tr_done[t - 1][NC - 1])
        p.e("vector", lambda e, t=t: e.tensor_scalar(out=xn32[:], in0=xtile[t % 2][:], scalar1=mmean[:, 0:1], scalar2=rinv[:, 0:1], op0=OP.subtract, op1=OP.mult), inc=cV)
        xn_done = p.v(cV)
        p.w("scalar", cV, xn_done)
        p.e("scalar", lambda e: e.copy(out=xnhi[:], in_=xn32[:]), inc=cS)
        hi_done = p.v(cS)
        p.w("vector", cS, hi_done)
        p.e("vector", lambda e: e.tensor_tensor(out=xnlo[:], in0=xn32[:], in1=xnhi[:], op=OP.subtract), inc=cV)
        ln_done[t] = p.v(cV)
        lnhi_done[t] = hi_done

    def p0_tr(t):
        p.w("tensor", cS, lnhi_done[t])
        p.w("tensor", cV, ln_done[t])
        for c in range(NC):
            slot = pt[c % 2]
            if c >= 2:
                p.w("tensor", cS, ahi_done[t][c - 2])
                p.w("tensor", cV, alo_done[t][c - 2])
            p.e("tensor", lambda e, t=t, c=c, slot=slot: e.transpose(slot[:, 0:128], xnhi[:, c * 128:(c + 1) * 128], idnbf[:]))
            p.e("tensor", lambda e, t=t, c=c, lslot=ptl[c % 2]: e.transpose(lslot[:, 0:128], xnlo[:, c * 128:(c + 1) * 128], idnbf[:]), inc=cT)
            tr_done[t][c] = p.v(cT)
            p.w("scalar", cT, tr_done[t][c])
            p.e("scalar", lambda e, t=t, c=c, slot=slot: e.copy(out=A[:, c * L + t * 128: c * L + (t + 1) * 128], in_=slot[:, 0:128]), inc=cS)
            ahi_done[t][c] = p.v(cS)
            if c >= 2:
                p.w("vector", cT, rt_done_c[c - 2])
            p.w("vector", cT, tr_done[t][c])
            p.e("vector", lambda e, t=t, c=c, lslot=ptl[c % 2]: e.tensor_copy(alo[c % 2][:], lslot[:, 0:128]), inc=cV)
            alo_done[t][c] = p.v(cV)
            p.w("tensor", cS, ahi_done[t][c])
            p.w("tensor", cV, alo_done[t][c])
            st = (c == 0)
            p.e("tensor", lambda e, t=t, c=c, st=st: e.matmul(plog[:, 0:NB], A[:, c * L + t * 128: c * L + (t + 1) * 128], wrthi[:, c * NB:(c + 1) * NB], start=st, stop=False))
            p.e("tensor", lambda e, t=t, c=c: e.matmul(plog[:, 0:NB], A[:, c * L + t * 128: c * L + (t + 1) * 128], wrtlo[:, c * NB:(c + 1) * NB], start=False, stop=False))
            p.e("tensor", lambda e, t=t, c=c: e.matmul(plog[:, 0:NB], alo[c % 2][:], wrthi[:, c * NB:(c + 1) * NB], start=False, stop=(c == NC - 1)), inc=cT)
            rt_done_c[c] = p.v(cT)
        rt_done[t] = p.v(cT)
        p.w("vector", cT, rt_done[t])
        p.w("vector", cS, brbc_done)
        p.e("vector", lambda e: e.tensor_tensor(out=logits[:], in0=plog[:, 0:NB], in1=brbc[:], op=OP.add), inc=cV)
        for i in range(NB):
            p.e("vector", lambda e, i=i: e.tensor_scalar(out=cmp0[:], in0=logits[:], scalar1=logits[:, i:i + 1], scalar2=0.0, op0=OP.is_gt, op1=OP.add, accum_out=cnt[:, i:i + 1]))
        p.e("vector", lambda e: e.tensor_scalar(out=maskb[:], in0=cnt[:], scalar1=3.5, scalar2=None, op0=OP.is_lt), inc=cV)
        mk_done = p.v(cV)
        p.w("tensor", cV, mk_done)
        if t >= 1:
            p.w("tensor", cS, mask_done[t - 1])
        p.e("tensor", lambda e: e.transpose(pmt[0:NB, 0:128], maskb[:], idnbf[:]), inc=cT)
        mt_done = p.v(cT)
        p.w("scalar", cT, mt_done)
        p.e("scalar", lambda e, t=t: e.copy(out=maskT[0:NB, t * 128:(t + 1) * 128], in_=pmt[0:NB, 0:128]), inc=cS)
        mask_done[t] = p.v(cS)

    p.w("tensor", cV, wrt_ready)
    p.dbar("tensor", dG)
    p.e("tensor", lambda e: e.matmul(plog[:, 0:NB], onesrf[0:1, :], br1[0:1, :], start=True, stop=True), inc=cT)
    brmm = p.v(cT)
    p.w("scalar", cT, brmm)
    p.e("scalar", lambda e: e.copy(out=brbc[:], in_=plog[:, 0:NB]), inc=cS)
    brbc_done = p.v(cS)
    p.w("tensor", cS, brbc_done)
    rt_done_c = [0] * NC
    for t in range(NT):
        p0_ln(t)
        p0_tr(t)
    p0_end_T = p.v(cT)
    p0_end_S = p.v(cS)
    p0_end_V = p.v(cV)
    p0.close()
    p.w("sync", cT, p0_end_T)
    p.w("sync", cS, p0_end_S)
    p.w("sync", cV, p0_end_V)

    p1 = ExitStack()
    wring = [p1.enter_context(nc.sbuf_tensor(f"wring{i}", [128, 512], F32)) for i in range(4)]
    wkb = [p1.enter_context(nc.sbuf_tensor(f"wkb{i}", [128, NC * 512], BF16)) for i in range(2)]
    wvb = [p1.enter_context(nc.sbuf_tensor(f"wvb{i}", [128, NC * 512], BF16)) for i in range(2)]
    ksb = [p1.enter_context(nc.sbuf_tensor(f"ksb{i}", [128, 512], BF16)) for i in range(2)]
    vsb = [p1.enter_context(nc.sbuf_tensor(f"vsb{i}", [128, 512], BF16)) for i in range(2)]
    kp = [p1.enter_context(nc.psum_tensor(f"kp{i}", [128, 512], F32)) for i in range(2)]
    vp = [p1.enter_context(nc.psum_tensor(f"vp{i}", [128, 512], F32)) for i in range(2)]
    kvp = p1.enter_context(nc.psum_tensor("kvp", [128, NH * DH], F32))

    wcast_done = [0, 0]
    wcons_done = [0, 0]
    kevac = [0, 0]
    vevac = [0, 0]
    kvmm = [0, 0]

    def stage_pair(ncx, j):
        par = ncx % 2
        if j < NC:
            i, w_d, dst = j, wk_d, wkb
        else:
            i, w_d, dst = j - NC, wv_d, wvb
        rs = (ncx * 2 * NC + j) % 4
        p.w("sync", cS, wring_cast[rs])
        dd = p.dma("sync", lambda e, i=i, ncx=ncx, rs=rs, w_d=w_d: e.dma_start(out=wring[rs][:], in_=w_d[i * 128:(i + 1) * 128, ncx * 512:(ncx + 1) * 512]), dS)
        p.wd("scalar", dd)
        if ncx >= 2 and j == 0:
            p.w("scalar", cT, wcons_done[par])
        p.e("scalar", lambda e, i=i, par=par, rs=rs, dst=dst: e.copy(out=dst[par][:, i * 512:(i + 1) * 512], in_=wring[rs][:]), inc=cS)
        wring_cast[rs] = p.v(cS)

    for j in range(2 * NC):
        stage_pair(0, j)
    wcast_done[0] = p.v(cS)
    for ncx in range(4):
        par = ncx % 2
        p.w("tensor", cS, wcast_done[par])

        kevac_t = [0] * NT
        vevac_t = [0] * NT
        kv_t = [0] * NT
        for t in range(NT):
            sl = t % 2
            if ncx < 3 and t >= 8:
                for j in range((t - 8) * 4, (t - 7) * 4):
                    stage_pair(ncx + 1, j)
                if t == NT - 1:
                    wcast_done[(ncx + 1) % 2] = p.v(cS)
            if t >= 2:
                p.w("tensor", cS, kevac_t[t - 2])
            for i in range(NC):
                p.e("tensor", lambda e, i=i, t=t, par=par, sl=sl: e.matmul(kp[sl][:], A[:, i * L + t * 128: i * L + (t + 1) * 128], wkb[par][:, i * 512:(i + 1) * 512], start=(i == 0), stop=(i == NC - 1)), inc=(cT if i == NC - 1 else None))
            kmm = p.v(cT)
            if t >= 2:
                p.w("tensor", cV, vevac_t[t - 2])
            for i in range(NC):
                p.e("tensor", lambda e, i=i, t=t, par=par, sl=sl: e.matmul(vp[sl][:], A[:, i * L + t * 128: i * L + (t + 1) * 128], wvb[par][:, i * 512:(i + 1) * 512], start=(i == 0), stop=(i == NC - 1)), inc=(cT if i == NC - 1 else None))
            vmm = p.v(cT)
            p.w("scalar", cT, kmm)
            if t >= 2:
                p.w("scalar", cT, kv_t[t - 2])
            p.e("scalar", lambda e, sl=sl: e.activation(out=ksb[sl][:], in_=kp[sl][:], func=AF.Relu, scale=INV_SCALE), inc=cS)
            kevac_t[t] = p.v(cS)
            p.w("vector", cT, vmm)
            if t >= 2:
                p.w("vector", cT, kv_t[t - 2])
            p.e("vector", lambda e, sl=sl: e.tensor_copy(vsb[sl][:], vp[sl][:]), inc=cV)
            vevac_t[t] = p.v(cV)
            p.w("tensor", cS, kevac_t[t])
            p.w("tensor", cV, vevac_t[t])
            for hh in range(4):
                h = ncx * 4 + hh
                p.e("tensor", lambda e, h=h, hh=hh, sl=sl, t=t: e.matmul(kvp[:, h * 128:(h + 1) * 128], ksb[sl][:, hh * 128:(hh + 1) * 128], vsb[sl][:, hh * 128:(hh + 1) * 128], start=(t == 0 and hh == 0), stop=(t == NT - 1 and hh == 3)), inc=(cT if hh == 3 else None))
            kv_t[t] = p.v(cT)
        wcons_done[par] = p.v(cT)
        p.w("vector", cT, wcons_done[par])
        p.e("vector", lambda e, ncx=ncx: e.tensor_copy(kvsb[:, ncx * 512:(ncx + 1) * 512], kvp[:, ncx * 512:(ncx + 1) * 512]), inc=cV)
    kv_ready = p.v(cV)
    p1_end_T = p.v(cT)
    p1_end_S = p.v(cS)
    p1.close()
    p.w("sync", cT, p1_end_T)
    p.w("sync", cS, p1_end_S)
    p.w("sync", cV, kv_ready)

    p3 = ExitStack()
    y_acc = p3.enter_context(nc.sbuf_tensor("y_acc", [128, 8 * D], F32))
    hm = p3.enter_context(nc.sbuf_tensor("hm", [128, 8 * 1024], BF16))
    w2b = [p3.enter_context(nc.sbuf_tensor(f"w2b{i}", [128, 8 * 1024], BF16)) for i in range(2)]
    w1ring = [p3.enter_context(nc.sbuf_tensor(f"w1ring{i}", [128, 1024], F32)) for i in range(2)]
    w1b = [p3.enter_context(nc.sbuf_tensor(f"w1b{i}", [128, 1024], BF16)) for i in range(2)]
    w2ring = [p3.enter_context(nc.sbuf_tensor(f"w2ring{i}", [128, 512], F32)) for i in range(2)]
    mbc = [p3.enter_context(nc.sbuf_tensor(f"mbc{i}", [128, 1024], BF16)) for i in range(2)]
    mrow = p3.enter_context(nc.sbuf_tensor("mrow0", [1, 512], BF16))
    hps = [p3.enter_context(nc.psum_tensor(f"hps{i}", [128, 1024], F32)) for i in range(2)]
    yps = [p3.enter_context(nc.psum_tensor(f"yps{i}", [128, 1024], F32)) for i in range(2)]

    w1dma = [0, 0]
    w1cast = [0, 0]
    w1cons = [0, 0]
    gelu_done = {}
    hmmask_done = {}
    gst_dma = [None, None]
    w2cast_slot = [0, 0]
    w2cast_slot_g = [0, 0]
    w2cons_fh = [0, 0]
    yadd_slot = [0, 0]
    mrow_guard = [0, 0]
    silu_mark = 0

    p.w("tensor", cV, kv_ready)
    for Hh in range(2):
        tok0 = Hh * 1024
        for slab in range(8):
            b0 = slab * 2
            if (Hh, slab) != (0, 0):
                p.w("tensor", cS, gelu_done[(Hh, slab - 1, 7) if slab > 0 else (Hh - 1, 7, 7)])
            p.w("tensor", cS, mask_done[NT - 1])
            for bb in range(2):
                b = b0 + bb
                if bb == 1:
                    p.w("tensor", cS, p.v(cS))
                for gg in range(2):
                    p.w("scalar", cS, mask_done[NT - 1])
                    p.w("scalar", cT, mrow_guard[0])
                    mrow_dma = p.dma("scalar", lambda e, b=b, gg=gg, tok0=tok0: e.dma_start(out=mrow[0:1, :], in_=maskT[b:b + 1, tok0 + gg * 512: tok0 + (gg + 1) * 512]), dV)
                    p.wd("tensor", mrow_dma)
                    p.e("tensor", lambda e, gg=gg: e.matmul(hps[0][:, gg * 512:(gg + 1) * 512], onesrb[0:1, :], mrow[0:1, :], start=True, stop=True), inc=cT)
                    mrow_guard[0] = p.v(cT)
                bcmm = p.v(cT)
                p.w("scalar", cT, bcmm)
                p.e("scalar", lambda e, bb=bb: e.copy(out=mbc[bb][:], in_=hps[0][:, 0:1024]), inc=cS)
            mbc_done = p.v(cS)
            p.w("tensor", cS, mbc_done)

            for jc in range(8):
                jg = slab * 8 + jc
                sl = jc % 2
                for hf in range(2):
                    p.w("sync", cS, w1cast[hf])
                    w1dma[hf] = p.dma("sync", lambda e, jg=jg, hf=hf: e.dma_start(out=w1ring[hf][:], in_=w1r_d[jg, :, hf * 1024:(hf + 1) * 1024]), dS)
                    p.wd("scalar", w1dma[hf])
                    p.w("scalar", cT, w1cons[hf])
                    p.e("scalar", lambda e, hf=hf: e.copy(out=w1b[hf][:], in_=w1ring[hf][:]), inc=cS)
                    w1cast[hf] = p.v(cS)
                if (Hh * 64 + jg) >= 1:
                    p.w("tensor", cS, gelu_done.get((Hh, slab, jc - 2), gelu_done.get((Hh, slab - 1, jc + 6), 0)))
                for hf in range(2):
                    p.w("tensor", cS, w1cast[hf])
                    for ng in range(2):
                        for ii in range(8):
                            i = hf * 8 + ii
                            p.e("tensor", lambda e, i=i, ii=ii, ng=ng, sl=sl, hf=hf, tok0=tok0: e.matmul(hps[sl][:, ng * 512:(ng + 1) * 512], w1b[hf][:, ii * 128:(ii + 1) * 128], A[:, i * L + tok0 + ng * 512: i * L + tok0 + (ng + 1) * 512], start=(i == 0), stop=(i == NC - 1)), inc=(cT if (i == NC - 1 and ng == 1) or (ii == 7 and ng == 1) else None))
                    w1cons[hf] = p.v(cT)
                hmm = p.v(cT)
                p.w("scalar", cT, hmm)
                p.e("scalar", lambda e, jc=jc, sl=sl: e.activation(out=hm[:, jc * 1024:(jc + 1) * 1024], in_=hps[sl][:], func=AF.Gelu), inc=cS)
                gelu_done[(Hh, slab, jc)] = p.v(cS)
                p.w("vector", cS, gelu_done[(Hh, slab, jc)])
                p.w("vector", cS, mbc_done)
                p.e("vector", lambda e, jc=jc, bb=jc // 4: e.tensor_tensor(out=hm[:, jc * 1024:(jc + 1) * 1024], in0=hm[:, jc * 1024:(jc + 1) * 1024], in1=mbc[bb][:], op=OP.mult), inc=cV)
                hmmask_done[(Hh, slab, jc)] = p.v(cV)
            hm_ready = p.v(cV)

            for fh in range(2):
                qeng = "sync" if fh == 0 else "gpsimd"
                qsem = dS if fh == 0 else dG
                for jc in range(8):
                    jg = slab * 8 + jc
                    for cc in range(2):
                        p.w(qeng, cS, w2cast_slot[cc])
                        dd = p.dma(qeng, lambda e, jg=jg, fh=fh, cc=cc: e.dma_start(out=w2ring[cc][:], in_=w2t_d[jg * 128:(jg + 1) * 128, fh * 1024 + cc * 512: fh * 1024 + (cc + 1) * 512]), qsem)
                        p.wd("scalar", dd)
                        if jc == 0 and cc == 0:
                            p.w("scalar", cT, w2cons_fh[fh])
                        p.e("scalar", lambda e, jc=jc, fh=fh, cc=cc: e.copy(out=w2b[fh][:, jc * 1024 + cc * 512: jc * 1024 + (cc + 1) * 512], in_=w2ring[cc][:]), inc=cS)
                        w2cast_slot[cc] = p.v(cS)
                w2_ready = p.v(cS)
                p.w("tensor", cV, hm_ready)
                p.w("tensor", cS, w2_ready)
                for t8 in range(8):
                    sl = t8 % 2
                    p.w("tensor", cV, yadd_slot[sl])
                    for jc in range(8):
                        for fg in range(2):
                            p.e("tensor", lambda e, t8=t8, jc=jc, fg=fg, fh=fh, sl=sl: e.matmul(yps[sl][:, fg * 512:(fg + 1) * 512], hm[:, jc * 1024 + t8 * 128: jc * 1024 + (t8 + 1) * 128], w2b[fh][:, jc * 1024 + fg * 512: jc * 1024 + (fg + 1) * 512], start=(jc == 0), stop=(jc == 7)), inc=(cT if (jc == 7 and fg == 1) else None))
                    ymm = p.v(cT)
                    p.w("vector", cT, ymm)
                    if slab == 0:
                        if Hh == 1:
                            p.w("vector", cS, silu_mark)
                            p.dbar("vector", dG)
                        p.e("vector", lambda e, t8=t8, fh=fh, sl=sl: e.tensor_copy(y_acc[:, t8 * D + fh * 1024: t8 * D + (fh + 1) * 1024], yps[sl][:]), inc=cV)
                    else:
                        p.e("vector", lambda e, t8=t8, fh=fh, sl=sl: e.tensor_tensor(out=y_acc[:, t8 * D + fh * 1024: t8 * D + (fh + 1) * 1024], in0=y_acc[:, t8 * D + fh * 1024: t8 * D + (fh + 1) * 1024], in1=yps[sl][:], op=OP.add), inc=cV)
                    yadd_slot[sl] = p.v(cV)
                w2cons_fh[fh] = p.v(cT)
        yfin = p.v(cV)
        for t8 in range(8):
            p.w("scalar", cV, yfin)
            p.e("scalar", lambda e, t8=t8: e.activation(out=y_acc[:, t8 * D:(t8 + 1) * D], in_=y_acc[:, t8 * D:(t8 + 1) * D], func=AF.Silu), inc=cS)
            gdone = p.v(cS)
            p.w("gpsimd", cS, gdone)
            gst_dma[0] = p.dma("gpsimd", lambda e, t8=t8, tok0=tok0: e.dma_start(out=g_d[tok0 + t8 * 128: tok0 + (t8 + 1) * 128, :], in_=y_acc[:, t8 * D:(t8 + 1) * D]), dG)
        silu_mark = p.v(cS)
    p3.close()

    p2o = ExitStack()
    osb = p2o.enter_context(nc.sbuf_tensor("osb", [128, NH * L], BF16))
    p2 = ExitStack()
    wring2 = [p2.enter_context(nc.sbuf_tensor(f"wring2{i}", [128, 512], F32)) for i in range(2)]
    wqb = [p2.enter_context(nc.sbuf_tensor(f"wqb{i}", [128, D], BF16)) for i in range(2)]
    wub = [p2.enter_context(nc.sbuf_tensor(f"wub{i}", [128, D], BF16)) for i in range(2)]
    qtsb = [p2.enter_context(nc.sbuf_tensor(f"qtsb{i}", [128, L], BF16)) for i in range(2)]
    utsb = [p2.enter_context(nc.sbuf_tensor(f"utsb{i}", [128, L], BF16)) for i in range(2)]
    attnsb = [p2.enter_context(nc.sbuf_tensor(f"attnsb{i}", [128, L], F32)) for i in range(2)]
    sq32 = p2.enter_context(nc.sbuf_tensor("sq32", [128, 1024], F32))
    sqh = p2.enter_context(nc.sbuf_tensor("sqh", [128, 1024], BF16))
    sql = p2.enter_context(nc.sbuf_tensor("sql", [128, 1024], BF16))
    srow = p2.enter_context(nc.sbuf_tensor("srow", [1, 512], F32))
    srow2 = p2.enter_context(nc.sbuf_tensor("srow2", [1, 512], F32))
    tmp512 = p2.enter_context(nc.sbuf_tensor("tmp512", [128, 512], F32))
    qp = [p2.enter_context(nc.psum_tensor(f"qp{i}", [128, 512], F32)) for i in range(2)]
    ap_ = [p2.enter_context(nc.psum_tensor(f"ap{i}", [128, 512], F32)) for i in range(2)]
    ssp = p2.enter_context(nc.psum_tensor("ssp", [128, 512], F32))
    bcp = [p2.enter_context(nc.psum_tensor(f"bcp{i}", [128, 512], F32)) for i in range(2)]

    wq_cast = [0, 0]
    wq_cons = [0, 0]
    q_evac = {}
    u_evac = {}
    o_done = {}
    au_done = {}
    qu_done = {}
    wring2_slot = [0, 0]

    def stage_qu(h):
        par = h % 2
        first_cast = True
        for wdst, wsrc in ((wqb, wqr_d), (wub, wur_d)):
            for hf in range(4):
                sslot = hf % 2
                p.w("gpsimd", cS, wring2_slot[sslot])
                wdma = p.dma("gpsimd", lambda e, h=h, hf=hf, sslot=sslot, wsrc=wsrc: e.dma_start(out=wring2[sslot][:], in_=wsrc[h, :, hf * 512:(hf + 1) * 512]), dG)
                p.wd("scalar", wdma)
                if first_cast and h >= 2:
                    p.w("scalar", cT, wq_cons[par])
                first_cast = False
                p.e("scalar", lambda e, par=par, hf=hf, sslot=sslot, wdst=wdst: e.copy(out=wdst[par][:, hf * 512:(hf + 1) * 512], in_=wring2[sslot][:]), inc=cS)
                wring2_slot[sslot] = p.v(cS)
        wq_cast[par] = p.v(cS)
        p.w("tensor", cS, wq_cast[par])
        if h == 0:
            p.w("tensor", cV, kv_ready)
        for ng in range(4):
            sl = ng % 2
            if ng >= 2:
                p.w("tensor", cS, q_evac[(h, ng - 2)])
            elif h >= 1:
                p.w("tensor", cS, u_evac[(h - 1, ng + 2)])
            for i in range(NC):
                p.e("tensor", lambda e, h=h, i=i, ng=ng, par=par, sl=sl: e.matmul(qp[sl][:], wqb[par][:, i * 128:(i + 1) * 128], A[:, i * L + ng * 512:i * L + (ng + 1) * 512], start=(i == 0), stop=(i == NC - 1)), inc=(cT if i == NC - 1 else None))
            qmm = p.v(cT)
            p.w("scalar", cT, qmm)
            p.e("scalar", lambda e, h=h, ng=ng, sl=sl, par=par: e.activation(out=qtsb[par][:, ng * 512:(ng + 1) * 512], in_=qp[sl][:], func=AF.Relu, scale=INV_SCALE), inc=cS)
            q_evac[(h, ng)] = p.v(cS)
        for ng in range(4):
            sl = ng % 2
            p.w("tensor", cS, u_evac[(h, ng - 2)] if ng >= 2 else q_evac[(h, 3)])
            for i in range(NC):
                p.e("tensor", lambda e, h=h, i=i, ng=ng, par=par, sl=sl: e.matmul(qp[sl][:], wub[par][:, i * 128:(i + 1) * 128], A[:, i * L + ng * 512:i * L + (ng + 1) * 512], start=(i == 0), stop=(i == NC - 1)), inc=(cT if i == NC - 1 else None))
            umm = p.v(cT)
            p.w("scalar", cT, umm)
            if ng == 0 and h >= 2:
                p.w("scalar", cV, au_done[h - 2])
            p.e("scalar", lambda e, h=h, ng=ng, sl=sl, par=par: e.activation(out=utsb[par][:, ng * 512:(ng + 1) * 512], in_=qp[sl][:], func=AF.Silu), inc=cS)
            u_evac[(h, ng)] = p.v(cS)
        wq_cons[par] = p.v(cT)
        qu_done[h] = p.v(cS)

    def attn_mms(h):
        par = h % 2
        p.w("tensor", cS, qu_done[h])
        au_done[h] = None
        for ng in range(4):
            sl = ng % 2
            if ng >= 2:
                p.w("tensor", cV, attn_evac[ng - 2])
            p.e("tensor", lambda e, h=h, ng=ng, sl=sl, par=par: e.matmul(ap_[sl][:], kvsb[:, h * 128:(h + 1) * 128], qtsb[par][:, ng * 512:(ng + 1) * 512], start=True, stop=True), inc=cT)
            amm = p.v(cT)
            p.w("vector", cT, amm)
            p.e("vector", lambda e, ng=ng, sl=sl, par=par: e.tensor_copy(attnsb[par][:, ng * 512:(ng + 1) * 512], ap_[sl][:]), inc=cV)
            attn_evac[ng] = p.v(cV)
            p.e("vector", lambda e, h=h, ng=ng, par=par: e.tensor_tensor(out=osb[:, h * L + ng * 512: h * L + (ng + 1) * 512], in0=attnsb[par][:, ng * 512:(ng + 1) * 512], in1=utsb[par][:, ng * 512:(ng + 1) * 512], op=OP.mult), inc=cV)
        au_done[h] = p.v(cV)

    def ssbc(h):
        par = h % 2
        for hh2 in range(2):
          p.e("vector", lambda e, par=par, hh2=hh2: e.tensor_tensor(out=sq32[:], in0=attnsb[par][:, hh2 * 1024:(hh2 + 1) * 1024], in1=attnsb[par][:, hh2 * 1024:(hh2 + 1) * 1024], op=OP.mult), inc=cV)
          sq_done = p.v(cV)
          p.w("scalar", cV, sq_done)
          p.e("scalar", lambda e: e.copy(out=sqh[:], in_=sq32[:]), inc=cS)
          sqh_done = p.v(cS)
          p.w("vector", cS, sqh_done)
          p.e("vector", lambda e: e.tensor_tensor(out=sql[:], in0=sq32[:], in1=sqh[:], op=OP.subtract), inc=cV)
          sql_done = p.v(cV)
          p.w("tensor", cV, sql_done)
          for ng in (2 * hh2, 2 * hh2 + 1):
            sl = ng % 2
            lo = (ng % 2) * 512
            p.e("tensor", lambda e, lo=lo: e.matmul(ssp[0:1, 0:512], onescol[:], sqh[:, lo:lo + 512], start=True, stop=False))
            p.e("tensor", lambda e, lo=lo: e.matmul(ssp[0:1, 0:512], onescol[:], sql[:, lo:lo + 512], start=False, stop=True), inc=cT)
            ssmm = p.v(cT)
            p.w("vector", cT, ssmm)
            p.e("vector", lambda e, ng=ng: e.tensor_scalar(out=srow[0:1, 0:512], in0=ssp[0:1, 0:512], scalar1=1.0 / 128.0, scalar2=1e-24, op0=OP.mult, op1=OP.max), inc=cV)
            ts_done = p.v(cV)
            p.w("scalar", cV, ts_done)
            p.e("scalar", lambda e: e.activation(out=srow2[0:1, 0:512], in_=srow[0:1, 0:512], func=AF.Ln), inc=cS)
            p.e("scalar", lambda e: e.activation(out=srow[0:1, 0:512], in_=srow2[0:1, 0:512], func=AF.Exp, scale=-0.5), inc=cS)
            rec_done = p.v(cS)
            p.w("tensor", cS, rec_done)
            if ng >= 2:
                p.w("tensor", cV, omul1[ng - 2])
            p.e("tensor", lambda e, ng=ng, sl=sl: e.matmul(bcp[sl][:], onesrf[0:1, :], srow[0:1, 0:512], start=True, stop=True), inc=cT)
            bcmm = p.v(cT)
            p.w("vector", cT, bcmm)
            p.e("vector", lambda e, h=h, ng=ng, sl=sl: e.tensor_tensor(out=osb[:, h * L + ng * 512: h * L + (ng + 1) * 512], in0=osb[:, h * L + ng * 512: h * L + (ng + 1) * 512], in1=bcp[sl][:], op=OP.mult), inc=cV)
            omul1[ng] = p.v(cV)
        o_done[h] = p.v(cV)

    for h in range(NH):
        stage_qu(h)
        if h >= 1:
            attn_mms(h - 1)
        if h >= 2:
            ssbc(h - 2)
    attn_mms(NH - 1)
    ssbc(NH - 2)
    ssbc(NH - 1)
    p2_end_V = p.v(cV)
    p2_end_S = p.v(cS)
    p2_end_T = p.v(cT)
    p2.close()

    p4 = ExitStack()
    wob = p4.enter_context(nc.sbuf_tensor("wob", [128, NC * 512], BF16))
    woring = [p4.enter_context(nc.sbuf_tensor(f"woring{i}", [128, 512], F32)) for i in range(2)]
    xst = [p4.enter_context(nc.sbuf_tensor(f"xst{i}", [128, 512], F32)) for i in range(2)]
    gst = [p4.enter_context(nc.sbuf_tensor(f"gst{i}", [128, 512], F32)) for i in range(2)]
    sgt = [p4.enter_context(nc.sbuf_tensor(f"sgt{i}", [128, 512], F32)) for i in range(2)]
    ost = [p4.enter_context(nc.sbuf_tensor(f"ost{i}", [128, 512], F32)) for i in range(2)]
    sps = [p4.enter_context(nc.psum_tensor(f"sps{i}", [128, 512], F32)) for i in range(2)]

    wo_slot = [0, 0]
    sg_slot = [0, 0]
    gadd_slot = [0, 0]
    out_slot = [None, None]
    smm_last = 0
    p.w("gpsimd", cV, p2_end_V)
    p.w("gpsimd", cS, p2_end_S)
    p.w("gpsimd", cT, p2_end_T)
    for f in range(4):
        for i in range(NC):
            rs = i % 2
            p.w("gpsimd", cS, wo_slot[rs])
            dd = p.dma("gpsimd", lambda e, i=i, f=f, rs=rs: e.dma_start(out=woring[rs][:], in_=wo_d[i * 128:(i + 1) * 128, f * 512:(f + 1) * 512]), dG)
            p.wd("scalar", dd)
            if f > 0 and i == 0:
                p.w("scalar", cT, smm_last)
            p.e("scalar", lambda e, i=i, rs=rs: e.copy(out=wob[:, i * 512:(i + 1) * 512], in_=woring[rs][:]), inc=cS)
            wo_slot[rs] = p.v(cS)
        wo_ready = p.v(cS)
        p.w("tensor", cS, wo_ready)
        for t in range(NT):
            sl = t % 2
            p.w("gpsimd", cG, gadd_slot[sl])
            p.w("gpsimd", cV, sg_slot[sl])
            if t == 0 and f == 0:
                p.dbar("gpsimd", dG)
            xtok = p.dma("gpsimd", lambda e, t=t, f=f, sl=sl: e.dma_start(out=xst[sl][:], in_=x_d[t * 128:(t + 1) * 128, f * 512:(f + 1) * 512]), dG)
            gtok = p.dma("gpsimd", lambda e, t=t, f=f, sl=sl: e.dma_start(out=gst[sl][:], in_=g_d[t * 128:(t + 1) * 128, f * 512:(f + 1) * 512]), dG)
            p.w("tensor", cV, sg_slot[sl])
            for i in range(NC):
                p.e("tensor", lambda e, i=i, t=t, sl=sl: e.matmul(sps[sl][:], osb[:, i * L + t * 128: i * L + (t + 1) * 128], wob[:, i * 512:(i + 1) * 512], start=(i == 0), stop=(i == NC - 1)), inc=(cT if i == NC - 1 else None))
            smm = p.v(cT)
            smm_last = smm
            p.w("vector", cT, smm)
            p.wd("vector", xtok)
            p.wd("vector", gtok)
            p.w("vector", cG, gadd_slot[sl])
            p.e("vector", lambda e, t=t, sl=sl: e.tensor_tensor(out=sgt[sl][:], in0=sps[sl][:], in1=gst[sl][:], op=OP.mult), inc=cV)
            sg_slot[sl] = p.v(cV)
            p.w("gpsimd", cV, sg_slot[sl])
            p.wd("gpsimd", out_slot[sl])
            p.e("gpsimd", lambda e, t=t, sl=sl: e.tensor_tensor(out=ost[sl][:], in0=sgt[sl][:], in1=xst[sl][:], op=OP.add), inc=cG)
            gadd_slot[sl] = p.v(cG)
            p.w("sync", cG, gadd_slot[sl])
            out_slot[sl] = p.dma("sync", lambda e, t=t, f=f, sl=sl: e.dma_start(out=out_d[t * 128:(t + 1) * 128, f * 512:(f + 1) * 512], in_=ost[sl][:]), dO)
    p4.close()
    p2o.close()

    p.dbar("sync", dO)

    with nc.Block() as blk:
        @blk.tensor
        def _(e):
            for f in p.st["tensor"]:
                f(e)

        @blk.vector
        def _(e):
            for f in p.st["vector"]:
                f(e)

        @blk.scalar
        def _(e):
            for f in p.st["scalar"]:
                f(e)

        @blk.gpsimd
        def _(e):
            for f in p.st["gpsimd"]:
                f(e)

        @blk.sync
        def _(e):
            for f in p.st["sync"]:
                f(e)
    es.close()
    return nc


wring_cast = [0, 0, 0, 0]
attn_evac = [0, 0, 0, 0]
omul1 = [0, 0, 0, 0]


def kernel(L0_attn_x, L0_attn_Wq, L0_attn_Wk, L0_attn_Wv, L0_attn_Wu, L0_attn_Wo,
           L0_ffn_Wr, L0_ffn_br, L0_ffn_W1, L0_ffn_W2):
    X = np.ascontiguousarray(L0_attn_x, dtype=np.float32)
    wq = np.ascontiguousarray(L0_attn_Wq, dtype=np.float32)
    wk = np.ascontiguousarray(L0_attn_Wk, dtype=np.float32)
    wv = np.ascontiguousarray(L0_attn_Wv, dtype=np.float32)
    wu = np.ascontiguousarray(L0_attn_Wu, dtype=np.float32)
    wo = np.ascontiguousarray(L0_attn_Wo, dtype=np.float32)
    wr = np.ascontiguousarray(L0_ffn_Wr, dtype=np.float32)
    br = np.ascontiguousarray(L0_ffn_br, dtype=np.float32)
    w1 = np.ascontiguousarray(L0_ffn_W1, dtype=np.float32)
    w2 = np.ascontiguousarray(L0_ffn_W2, dtype=np.float32)

    wqr = np.ascontiguousarray(wq.reshape(NC, 128, NH, 128).transpose(2, 1, 0, 3).reshape(NH, 128, D))
    wur = np.ascontiguousarray(wu.reshape(NC, 128, NH, 128).transpose(2, 1, 0, 3).reshape(NH, 128, D))
    w1t = np.ascontiguousarray(w1.T)
    w1r = np.ascontiguousarray(w1t.reshape(NC, 128, 64, 128).transpose(2, 1, 0, 3).reshape(64, 128, D))
    w2t = np.ascontiguousarray(w2.T)
    wrt = np.ascontiguousarray(wr.T)
    wrr = np.ascontiguousarray(wrt.reshape(NC, 128, NB).transpose(1, 0, 2).reshape(128, NC * NB))
    brv = np.ascontiguousarray(br.reshape(1, NB))
    idnbf = np.eye(128, dtype=bfloat16)
    onescol = np.ones((128, 1), dtype=bfloat16)
    onesrb = np.ones((1, 128), dtype=bfloat16)
    onesrf = np.ones((1, 128), dtype=np.float32)

    nc = bass.Bass("TRN2", target_bir_lowering=False)
    build(nc)

    shared = dict(wk=wk, wv=wv, wo=wo, wqr=wqr, wur=wur, w1r=w1r, w2t=w2t,
                  wrr=wrr, brd=brv, idnbf=idnbf, onescol=onescol,
                  onesrb=onesrb, onesrf=onesrf,
                  cst=np.tile(np.array([[1e-5, INV_SCALE]], dtype=np.float32), (128, 1)))
    in_maps = [dict(shared, x=np.ascontiguousarray(X[b])) for b in range(8)]
    res = run_bass_kernel_spmd(nc, in_maps, core_ids=list(range(8)))
    return np.stack([np.asarray(r["out"], dtype=np.float32) for r in res.results], axis=0)


# version 12
# speedup vs baseline: 1.0745x; 1.0080x over previous
import math
import numpy as np
from ml_dtypes import bfloat16
from contextlib import ExitStack

import concourse.bass as bass
import concourse.mybir as mybir
from concourse.bass_utils import run_bass_kernel_spmd

F32 = mybir.dt.float32
BF16 = mybir.dt.bfloat16
AF = mybir.ActivationFunctionType
OP = mybir.AluOpType
AX = mybir.AxisListType

D = 2048
L = 2048
NH = 16
DH = 128
NB = 16
BS = 512
JD = 8192
NT = L // 128
NC = D // 128
INV_SCALE = 1.0 / math.sqrt(128.0)

ENGS = ("tensor", "vector", "scalar", "gpsimd", "sync")


class Prog:
    def __init__(self, nc, es):
        self.nc = nc
        self.es = es
        self.st = {k: [] for k in ENGS}
        self.sems = {}
        self.val = {}

    NSL = 8

    def sem(self, name):
        self.sems[name] = self.es.enter_context(self.nc.semaphore(name))
        self.val[name] = 0
        return name

    def dsem(self, name):
        for i in range(self.NSL):
            self.sem(f"{name}{i}")
        self.val[name + "_i"] = 0
        return name

    def dma(self, eng, fn, q):
        i = self.val[q + "_i"] % self.NSL
        self.val[q + "_i"] += 1
        sub = f"{q}{i}"
        self.w(eng, sub, self.val[sub])
        self.val[sub] += 16
        h = self.sems[sub]
        self.st[eng].append(lambda e, fn=fn, h=h: fn(e).then_inc(h, 16))
        return (sub, self.val[sub])

    def wd(self, eng, tok):
        if tok is not None:
            self.w(eng, tok[0], tok[1])

    def dbar(self, eng, q):
        for i in range(self.NSL):
            sub = f"{q}{i}"
            self.w(eng, sub, self.val[sub])

    SELF = {"vector": "cV", "scalar": "cS", "gpsimd": "cG"}

    def w(self, eng, sem, v=None):
        vv = self.val[sem] if v is None else v
        h = self.sems[sem]
        self.st[eng].append(lambda e, h=h, vv=vv: e.wait_ge(h, vv))

    def e(self, eng, fn, inc=None, k=1, dma=False):
        if not dma and eng in self.SELF:
            own = self.SELF[eng]
            self.w(eng, own, self.val[own])
            inc, k = own, 1
        if inc is not None:
            h = self.sems[inc]
            self.val[inc] += k
            self.st[eng].append(lambda e, fn=fn, h=h, k=k: fn(e).then_inc(h, k))
        else:
            self.st[eng].append(lambda e, fn=fn: fn(e))

    def v(self, sem):
        return self.val[sem]


def build(nc):
    es = ExitStack()
    p = Prog(nc, es)
    x_d = nc.dram_tensor("x", [L, D], F32, kind="ExternalInput").ap()
    wk_d = nc.dram_tensor("wk", [D, D], F32, kind="ExternalInput").ap()
    wv_d = nc.dram_tensor("wv", [D, D], F32, kind="ExternalInput").ap()
    wo_d = nc.dram_tensor("wo", [D, D], F32, kind="ExternalInput").ap()
    wqr_d = nc.dram_tensor("wqr", [NH, 128, D], F32, kind="ExternalInput").ap()
    wur_d = nc.dram_tensor("wur", [NH, 128, D], F32, kind="ExternalInput").ap()
    w1r_d = nc.dram_tensor("w1r", [JD // 128, 128, D], F32, kind="ExternalInput").ap()
    w2t_d = nc.dram_tensor("w2t", [JD, D], F32, kind="ExternalInput").ap()
    wrr_d = nc.dram_tensor("wrr", [128, NC * NB], F32, kind="ExternalInput").ap()
    br_d = nc.dram_tensor("brd", [1, NB], F32, kind="ExternalInput").ap()
    cst_d = nc.dram_tensor("cst", [128, 2], F32, kind="ExternalInput").ap()
    idnbf_d = nc.dram_tensor("idnbf", [128, 128], BF16, kind="ExternalInput").ap()
    onescol_d = nc.dram_tensor("onescol", [128, 1], BF16, kind="ExternalInput").ap()
    onesrb_d = nc.dram_tensor("onesrb", [1, 128], BF16, kind="ExternalInput").ap()
    onesrf_d = nc.dram_tensor("onesrf", [1, 128], F32, kind="ExternalInput").ap()
    out_d = nc.dram_tensor("out", [L, D], F32, kind="ExternalOutput").ap()
    g_d = nc.dram_tensor("gscr", [L, D], F32, kind="Internal").ap()

    dS = p.dsem("dS")
    dG = p.dsem("dG")
    dO = p.dsem("dO")
    dV = p.dsem("dV")
    cT = p.sem("cT")
    cV = p.sem("cV")
    cS = p.sem("cS")
    cG = p.sem("cG")

    A = es.enter_context(nc.sbuf_tensor("A", [128, NC * L], BF16))
    kvsb = es.enter_context(nc.sbuf_tensor("kvsb", [128, NH * DH], BF16))
    maskT = es.enter_context(nc.sbuf_tensor("maskT", [128, L], BF16))
    idnbf = es.enter_context(nc.sbuf_tensor("idnbfs", [128, 128], BF16))
    onescol = es.enter_context(nc.sbuf_tensor("onescols", [128, 1], BF16))
    onesrb = es.enter_context(nc.sbuf_tensor("onesrbs", [1, 128], BF16))
    onesrf = es.enter_context(nc.sbuf_tensor("onesrfs", [1, 128], F32))
    wrthi = es.enter_context(nc.sbuf_tensor("wrthi", [128, NC * NB], BF16))
    wrtlo = es.enter_context(nc.sbuf_tensor("wrtlo", [128, NC * NB], BF16))
    brbc = es.enter_context(nc.sbuf_tensor("brbc", [128, NB], F32))
    br1 = es.enter_context(nc.sbuf_tensor("br1", [1, NB], F32))
    cst = es.enter_context(nc.sbuf_tensor("cstsb", [128, 2], F32))
    nc.const_aps.register(F32, 1e-5, cst[:, 0:1])
    nc.const_aps.register(F32, INV_SCALE, cst[:, 1:2])

    for dst, src in ((idnbf, idnbf_d), (onescol, onescol_d),
                     (onesrb, onesrb_d), (onesrf, onesrf_d),
                     (br1, br_d), (cst, cst_d)):
        p.dma("gpsimd", lambda e, dst=dst, src=src: e.dma_start(out=dst[:], in_=src[:]), dG)


    p0 = ExitStack()
    wrt32 = p0.enter_context(nc.sbuf_tensor("wrt32", [128, NC * NB], F32))
    p.dma("gpsimd", lambda e: e.dma_start(out=wrt32[:], in_=wrr_d[:]), dG)
    p.dbar("vector", dG)
    p.e("vector", lambda e: e.tensor_copy(wrthi[:], wrt32[:]), inc=cV)
    p.e("vector", lambda e: e.tensor_tensor(wrtlo[:], wrt32[:], wrthi[:], OP.subtract), inc=cV)
    wrt_ready = p.v(cV)
    xtile = [p0.enter_context(nc.sbuf_tensor(f"xtile{i}", [128, D], F32)) for i in range(2)]
    xn32 = p0.enter_context(nc.sbuf_tensor("xn32", [128, D], F32))
    xnhi = p0.enter_context(nc.sbuf_tensor("xnhi", [128, D], BF16))
    xnlo = p0.enter_context(nc.sbuf_tensor("xnlo", [128, D], BF16))
    sqsc = p0.enter_context(nc.sbuf_tensor("sqsc", [128, D], F32))
    sumx = p0.enter_context(nc.sbuf_tensor("sumx", [128, 1], F32))
    ssq = p0.enter_context(nc.sbuf_tensor("ssq", [128, 1], F32))
    mmean = p0.enter_context(nc.sbuf_tensor("mmean", [128, 1], F32))
    m2 = p0.enter_context(nc.sbuf_tensor("m2", [128, 1], F32))
    varb = p0.enter_context(nc.sbuf_tensor("varb", [128, 1], F32))
    stdb = p0.enter_context(nc.sbuf_tensor("stdb", [128, 1], F32))
    rinv = p0.enter_context(nc.sbuf_tensor("rinv", [128, 1], F32))
    alo = [p0.enter_context(nc.sbuf_tensor(f"alo{i}", [128, 128], BF16)) for i in range(2)]
    logits = p0.enter_context(nc.sbuf_tensor("logits", [128, NB], F32))
    cmp0 = p0.enter_context(nc.sbuf_tensor("cmp0", [128, NB], F32))
    cnt = p0.enter_context(nc.sbuf_tensor("cnt", [128, NB], F32))
    maskb = p0.enter_context(nc.sbuf_tensor("maskb", [128, NB], BF16))
    pt = [p0.enter_context(nc.psum_tensor(f"pt{i}", [128, 128], BF16)) for i in range(2)]
    ptl = [p0.enter_context(nc.psum_tensor(f"ptl{i}", [128, 128], BF16)) for i in range(2)]
    plog = p0.enter_context(nc.psum_tensor("plog", [128, 512], F32))
    pmt = p0.enter_context(nc.psum_tensor("pmt", [128, 128], BF16))

    xdma_done = [None] * NT
    ln_done = [0] * NT
    lnhi_done = [0] * NT
    tr_done = [[0] * NC for _ in range(NT)]
    ahi_done = [[0] * NC for _ in range(NT)]
    alo_done = [[0] * NC for _ in range(NT)]
    rt_done = [0] * NT
    mask_done = [0] * NT

    def p0_ln(t):
        if t >= 2:
            p.w("gpsimd", cV, ln_done[t - 2])
        xdma_done[t] = p.dma("gpsimd", lambda e, t=t: e.dma_start(out=xtile[t % 2][:], in_=x_d[t * 128:(t + 1) * 128, :]), dG)
        p.wd("vector", xdma_done[t])
        p.e("vector", lambda e, t=t: e.reduce_sum(sumx[:], xtile[t % 2][:], axis=AX.X), inc=cV)
        p.wd("scalar", xdma_done[t])
        if t >= 1:
            p.w("scalar", cV, ln_done[t - 1])
        p.e("scalar", lambda e, t=t: e.activation(out=sqsc[:], in_=xtile[t % 2][:], func=AF.Square, accum_out=ssq[:, 0:1]), inc=cS)
        sq_done = p.v(cS)
        p.e("vector", lambda e: e.tensor_scalar(out=mmean[:], in0=sumx[:], scalar1=1.0 / D, scalar2=None, op0=OP.mult))
        p.e("vector", lambda e: e.tensor_tensor(out=m2[:], in0=mmean[:], in1=mmean[:], op=OP.mult))
        p.w("vector", cS, sq_done)
        p.e("vector", lambda e: e.tensor_scalar(out=varb[:], in0=ssq[:], scalar1=1.0 / D, scalar2=m2[:, 0:1], op0=OP.mult, op1=OP.subtract), inc=cV)
        var_done = p.v(cV)
        p.w("scalar", cV, var_done)
        p.e("scalar", lambda e: e.activation(out=stdb[:], in_=varb[:], func=AF.Sqrt, bias=1e-5), inc=cS)
        std_done = p.v(cS)
        p.w("vector", cS, std_done)
        p.e("vector", lambda e: e.reciprocal(out=rinv[:], in_=stdb[:]))
        if t >= 1:
            p.w("vector", cT, tr_done[t - 1][NC - 1])
        p.e("vector", lambda e, t=t: e.tensor_scalar(out=xn32[:], in0=xtile[t % 2][:], scalar1=mmean[:, 0:1], scalar2=rinv[:, 0:1], op0=OP.subtract, op1=OP.mult), inc=cV)
        xn_done = p.v(cV)
        p.w("scalar", cV, xn_done)
        p.e("scalar", lambda e: e.copy(out=xnhi[:], in_=xn32[:]), inc=cS)
        hi_done = p.v(cS)
        p.w("vector", cS, hi_done)
        p.e("vector", lambda e: e.tensor_tensor(out=xnlo[:], in0=xn32[:], in1=xnhi[:], op=OP.subtract), inc=cV)
        ln_done[t] = p.v(cV)
        lnhi_done[t] = hi_done

    def p0_tr(t):
        p.w("tensor", cS, lnhi_done[t])
        p.w("tensor", cV, ln_done[t])
        for c in range(NC):
            slot = pt[c % 2]
            if c >= 2:
                p.w("tensor", cS, ahi_done[t][c - 2])
                p.w("tensor", cV, alo_done[t][c - 2])
            p.e("tensor", lambda e, t=t, c=c, slot=slot: e.transpose(slot[:, 0:128], xnhi[:, c * 128:(c + 1) * 128], idnbf[:]))
            p.e("tensor", lambda e, t=t, c=c, lslot=ptl[c % 2]: e.transpose(lslot[:, 0:128], xnlo[:, c * 128:(c + 1) * 128], idnbf[:]), inc=cT)
            tr_done[t][c] = p.v(cT)
            p.w("scalar", cT, tr_done[t][c])
            p.e("scalar", lambda e, t=t, c=c, slot=slot: e.copy(out=A[:, c * L + t * 128: c * L + (t + 1) * 128], in_=slot[:, 0:128]), inc=cS)
            ahi_done[t][c] = p.v(cS)
            if c >= 2:
                p.w("vector", cT, rt_done_c[c - 2])
            p.w("vector", cT, tr_done[t][c])
            p.e("vector", lambda e, t=t, c=c, lslot=ptl[c % 2]: e.tensor_copy(alo[c % 2][:], lslot[:, 0:128]), inc=cV)
            alo_done[t][c] = p.v(cV)
            p.w("tensor", cS, ahi_done[t][c])
            p.w("tensor", cV, alo_done[t][c])
            st = (c == 0)
            p.e("tensor", lambda e, t=t, c=c, st=st: e.matmul(plog[:, 0:NB], A[:, c * L + t * 128: c * L + (t + 1) * 128], wrthi[:, c * NB:(c + 1) * NB], start=st, stop=False))
            p.e("tensor", lambda e, t=t, c=c: e.matmul(plog[:, 0:NB], A[:, c * L + t * 128: c * L + (t + 1) * 128], wrtlo[:, c * NB:(c + 1) * NB], start=False, stop=False))
            p.e("tensor", lambda e, t=t, c=c: e.matmul(plog[:, 0:NB], alo[c % 2][:], wrthi[:, c * NB:(c + 1) * NB], start=False, stop=(c == NC - 1)), inc=cT)
            rt_done_c[c] = p.v(cT)
        rt_done[t] = p.v(cT)
        p.w("vector", cT, rt_done[t])
        p.w("vector", cS, brbc_done)
        p.e("vector", lambda e: e.tensor_tensor(out=logits[:], in0=plog[:, 0:NB], in1=brbc[:], op=OP.add), inc=cV)
        for i in range(NB):
            p.e("vector", lambda e, i=i: e.tensor_scalar(out=cmp0[:], in0=logits[:], scalar1=logits[:, i:i + 1], scalar2=0.0, op0=OP.is_gt, op1=OP.add, accum_out=cnt[:, i:i + 1]))
        p.e("vector", lambda e: e.tensor_scalar(out=maskb[:], in0=cnt[:], scalar1=3.5, scalar2=None, op0=OP.is_lt), inc=cV)
        mk_done = p.v(cV)
        p.w("tensor", cV, mk_done)
        if t >= 1:
            p.w("tensor", cS, mask_done[t - 1])
        p.e("tensor", lambda e: e.transpose(pmt[0:NB, 0:128], maskb[:], idnbf[:]), inc=cT)
        mt_done = p.v(cT)
        p.w("scalar", cT, mt_done)
        p.e("scalar", lambda e, t=t: e.copy(out=maskT[0:NB, t * 128:(t + 1) * 128], in_=pmt[0:NB, 0:128]), inc=cS)
        mask_done[t] = p.v(cS)

    p.w("tensor", cV, wrt_ready)
    p.dbar("tensor", dG)
    p.e("tensor", lambda e: e.matmul(plog[:, 0:NB], onesrf[0:1, :], br1[0:1, :], start=True, stop=True), inc=cT)
    brmm = p.v(cT)
    p.w("scalar", cT, brmm)
    p.e("scalar", lambda e: e.copy(out=brbc[:], in_=plog[:, 0:NB]), inc=cS)
    brbc_done = p.v(cS)
    p.w("tensor", cS, brbc_done)
    rt_done_c = [0] * NC
    for t in range(NT):
        p0_ln(t)
        p0_tr(t)
    p0_end_T = p.v(cT)
    p0_end_S = p.v(cS)
    p0_end_V = p.v(cV)
    p0.close()
    p.w("sync", cT, p0_end_T)
    p.w("sync", cS, p0_end_S)
    p.w("sync", cV, p0_end_V)

    p1 = ExitStack()
    wring = [p1.enter_context(nc.sbuf_tensor(f"wring{i}", [128, 512], F32)) for i in range(4)]
    wkb = [p1.enter_context(nc.sbuf_tensor(f"wkb{i}", [128, NC * 512], BF16)) for i in range(2)]
    wvb = [p1.enter_context(nc.sbuf_tensor(f"wvb{i}", [128, NC * 512], BF16)) for i in range(2)]
    ksb = [p1.enter_context(nc.sbuf_tensor(f"ksb{i}", [128, 512], BF16)) for i in range(2)]
    vsb = [p1.enter_context(nc.sbuf_tensor(f"vsb{i}", [128, 512], BF16)) for i in range(2)]
    kp = [p1.enter_context(nc.psum_tensor(f"kp{i}", [128, 512], F32)) for i in range(2)]
    vp = [p1.enter_context(nc.psum_tensor(f"vp{i}", [128, 512], F32)) for i in range(2)]
    kvp = p1.enter_context(nc.psum_tensor("kvp", [128, NH * DH], F32))

    wcast_done = [0, 0]
    wcons_done = [0, 0]
    kevac = [0, 0]
    vevac = [0, 0]
    kvmm = [0, 0]

    def stage_pair(ncx, j):
        par = ncx % 2
        if j < NC:
            i, w_d, dst = j, wk_d, wkb
        else:
            i, w_d, dst = j - NC, wv_d, wvb
        rs = (ncx * 2 * NC + j) % 4
        p.w("sync", cS, wring_cast[rs])
        dd = p.dma("sync", lambda e, i=i, ncx=ncx, rs=rs, w_d=w_d: e.dma_start(out=wring[rs][:], in_=w_d[i * 128:(i + 1) * 128, ncx * 512:(ncx + 1) * 512]), dS)
        p.wd("scalar", dd)
        if ncx >= 2 and j == 0:
            p.w("scalar", cT, wcons_done[par])
        p.e("scalar", lambda e, i=i, par=par, rs=rs, dst=dst: e.copy(out=dst[par][:, i * 512:(i + 1) * 512], in_=wring[rs][:]), inc=cS)
        wring_cast[rs] = p.v(cS)

    for j in range(2 * NC):
        stage_pair(0, j)
    wcast_done[0] = p.v(cS)
    for ncx in range(4):
        par = ncx % 2
        p.w("tensor", cS, wcast_done[par])

        kevac_t = [0] * NT
        vevac_t = [0] * NT
        kv_t = [0] * NT
        for t in range(NT):
            sl = t % 2
            if ncx < 3 and t >= 8:
                for j in range((t - 8) * 4, (t - 7) * 4):
                    stage_pair(ncx + 1, j)
                if t == NT - 1:
                    wcast_done[(ncx + 1) % 2] = p.v(cS)
            if t >= 2:
                p.w("tensor", cS, kevac_t[t - 2])
            for i in range(NC):
                p.e("tensor", lambda e, i=i, t=t, par=par, sl=sl: e.matmul(kp[sl][:], A[:, i * L + t * 128: i * L + (t + 1) * 128], wkb[par][:, i * 512:(i + 1) * 512], start=(i == 0), stop=(i == NC - 1)), inc=(cT if i == NC - 1 else None))
            kmm = p.v(cT)
            if t >= 2:
                p.w("tensor", cV, vevac_t[t - 2])
            for i in range(NC):
                p.e("tensor", lambda e, i=i, t=t, par=par, sl=sl: e.matmul(vp[sl][:], A[:, i * L + t * 128: i * L + (t + 1) * 128], wvb[par][:, i * 512:(i + 1) * 512], start=(i == 0), stop=(i == NC - 1)), inc=(cT if i == NC - 1 else None))
            vmm = p.v(cT)
            p.w("scalar", cT, kmm)
            if t >= 2:
                p.w("scalar", cT, kv_t[t - 2])
            p.e("scalar", lambda e, sl=sl: e.activation(out=ksb[sl][:], in_=kp[sl][:], func=AF.Relu, scale=INV_SCALE), inc=cS)
            kevac_t[t] = p.v(cS)
            p.w("vector", cT, vmm)
            if t >= 2:
                p.w("vector", cT, kv_t[t - 2])
            p.e("vector", lambda e, sl=sl: e.tensor_copy(vsb[sl][:], vp[sl][:]), inc=cV)
            vevac_t[t] = p.v(cV)
            p.w("tensor", cS, kevac_t[t])
            p.w("tensor", cV, vevac_t[t])
            for hh in range(4):
                h = ncx * 4 + hh
                p.e("tensor", lambda e, h=h, hh=hh, sl=sl, t=t: e.matmul(kvp[:, h * 128:(h + 1) * 128], ksb[sl][:, hh * 128:(hh + 1) * 128], vsb[sl][:, hh * 128:(hh + 1) * 128], start=(t == 0 and hh == 0), stop=(t == NT - 1 and hh == 3)), inc=(cT if hh == 3 else None))
            kv_t[t] = p.v(cT)
        wcons_done[par] = p.v(cT)
        p.w("vector", cT, wcons_done[par])
        p.e("vector", lambda e, ncx=ncx: e.tensor_copy(kvsb[:, ncx * 512:(ncx + 1) * 512], kvp[:, ncx * 512:(ncx + 1) * 512]), inc=cV)
    kv_ready = p.v(cV)
    p1_end_T = p.v(cT)
    p1_end_S = p.v(cS)
    p1.close()
    p.w("sync", cT, p1_end_T)
    p.w("sync", cS, p1_end_S)
    p.w("sync", cV, kv_ready)

    p3 = ExitStack()
    y_acc = p3.enter_context(nc.sbuf_tensor("y_acc", [128, 8 * D], F32))
    hm = p3.enter_context(nc.sbuf_tensor("hm", [128, 8 * 1024], BF16))
    w2b = [p3.enter_context(nc.sbuf_tensor(f"w2b{i}", [128, 8 * 1024], BF16)) for i in range(2)]
    w1ring = [p3.enter_context(nc.sbuf_tensor(f"w1ring{i}", [128, 1024], F32)) for i in range(2)]
    w1b = [p3.enter_context(nc.sbuf_tensor(f"w1b{i}", [128, 1024], BF16)) for i in range(2)]
    w2ring = [p3.enter_context(nc.sbuf_tensor(f"w2ring{i}", [128, 512], F32)) for i in range(2)]
    mbc = [p3.enter_context(nc.sbuf_tensor(f"mbc{i}", [128, 1024], BF16)) for i in range(2)]
    mrow = p3.enter_context(nc.sbuf_tensor("mrow0", [1, 512], BF16))
    hps = [p3.enter_context(nc.psum_tensor(f"hps{i}", [128, 1024], F32)) for i in range(2)]
    yps = [p3.enter_context(nc.psum_tensor(f"yps{i}", [128, 1024], F32)) for i in range(2)]

    w1dma = [0, 0]
    w1cast = [0, 0]
    w1cons = [0, 0]
    gelu_done = {}
    hmmask_done = {}
    gst_dma = [None, None]
    w2cast_slot = [0, 0]
    w2cast_slot_g = [0, 0]
    w2cons_fh = [0, 0]
    yadd_slot = [0, 0]
    mrow_guard = [0, 0]
    silu_mark = 0

    p.w("tensor", cV, kv_ready)
    for Hh in range(2):
        tok0 = Hh * 1024
        for slab in range(8):
            b0 = slab * 2
            if (Hh, slab) != (0, 0):
                p.w("tensor", cS, gelu_done[(Hh, slab - 1, 7) if slab > 0 else (Hh - 1, 7, 7)])
            p.w("tensor", cS, mask_done[NT - 1])
            for bb in range(2):
                b = b0 + bb
                if bb == 1:
                    p.w("tensor", cS, p.v(cS))
                for gg in range(2):
                    p.w("scalar", cS, mask_done[NT - 1])
                    p.w("scalar", cT, mrow_guard[0])
                    mrow_dma = p.dma("scalar", lambda e, b=b, gg=gg, tok0=tok0: e.dma_start(out=mrow[0:1, :], in_=maskT[b:b + 1, tok0 + gg * 512: tok0 + (gg + 1) * 512]), dV)
                    p.wd("tensor", mrow_dma)
                    p.e("tensor", lambda e, gg=gg: e.matmul(hps[0][:, gg * 512:(gg + 1) * 512], onesrb[0:1, :], mrow[0:1, :], start=True, stop=True), inc=cT)
                    mrow_guard[0] = p.v(cT)
                bcmm = p.v(cT)
                p.w("scalar", cT, bcmm)
                p.e("scalar", lambda e, bb=bb: e.copy(out=mbc[bb][:], in_=hps[0][:, 0:1024]), inc=cS)
            mbc_done = p.v(cS)
            p.w("tensor", cS, mbc_done)

            for jc in range(8):
                jg = slab * 8 + jc
                sl = jc % 2
                for hf in range(2):
                    p.w("sync", cS, w1cast[hf])
                    w1dma[hf] = p.dma("sync", lambda e, jg=jg, hf=hf: e.dma_start(out=w1ring[hf][:], in_=w1r_d[jg, :, hf * 1024:(hf + 1) * 1024]), dS)
                    p.wd("scalar", w1dma[hf])
                    p.w("scalar", cT, w1cons[hf])
                    p.e("scalar", lambda e, hf=hf: e.copy(out=w1b[hf][:], in_=w1ring[hf][:]), inc=cS)
                    w1cast[hf] = p.v(cS)
                if (Hh * 64 + jg) >= 1:
                    p.w("tensor", cS, gelu_done.get((Hh, slab, jc - 2), gelu_done.get((Hh, slab - 1, jc + 6), 0)))
                for hf in range(2):
                    p.w("tensor", cS, w1cast[hf])
                    for ng in range(2):
                        for ii in range(8):
                            i = hf * 8 + ii
                            p.e("tensor", lambda e, i=i, ii=ii, ng=ng, sl=sl, hf=hf, tok0=tok0: e.matmul(hps[sl][:, ng * 512:(ng + 1) * 512], w1b[hf][:, ii * 128:(ii + 1) * 128], A[:, i * L + tok0 + ng * 512: i * L + tok0 + (ng + 1) * 512], start=(i == 0), stop=(i == NC - 1)), inc=(cT if (i == NC - 1 and ng == 1) or (ii == 7 and ng == 1) else None))
                    w1cons[hf] = p.v(cT)
                hmm = p.v(cT)
                p.w("scalar", cT, hmm)
                p.e("scalar", lambda e, jc=jc, sl=sl: e.activation(out=hm[:, jc * 1024:(jc + 1) * 1024], in_=hps[sl][:], func=AF.Gelu), inc=cS)
                gelu_done[(Hh, slab, jc)] = p.v(cS)
                p.w("vector", cS, gelu_done[(Hh, slab, jc)])
                p.w("vector", cS, mbc_done)
                p.e("vector", lambda e, jc=jc, bb=jc // 4: e.tensor_tensor(out=hm[:, jc * 1024:(jc + 1) * 1024], in0=hm[:, jc * 1024:(jc + 1) * 1024], in1=mbc[bb][:], op=OP.mult), inc=cV)
                hmmask_done[(Hh, slab, jc)] = p.v(cV)
            hm_ready = p.v(cV)

            for fh in range(2):
                qeng = "sync" if fh == 0 else "gpsimd"
                qsem = dS if fh == 0 else dG
                for jc in range(8):
                    jg = slab * 8 + jc
                    for cc in range(2):
                        p.w(qeng, cS, w2cast_slot[cc])
                        dd = p.dma(qeng, lambda e, jg=jg, fh=fh, cc=cc: e.dma_start(out=w2ring[cc][:], in_=w2t_d[jg * 128:(jg + 1) * 128, fh * 1024 + cc * 512: fh * 1024 + (cc + 1) * 512]), qsem)
                        p.wd("scalar", dd)
                        if jc == 0 and cc == 0:
                            p.w("scalar", cT, w2cons_fh[fh])
                        p.e("scalar", lambda e, jc=jc, fh=fh, cc=cc: e.copy(out=w2b[fh][:, jc * 1024 + cc * 512: jc * 1024 + (cc + 1) * 512], in_=w2ring[cc][:]), inc=cS)
                        w2cast_slot[cc] = p.v(cS)
                w2_ready = p.v(cS)
                p.w("tensor", cV, hm_ready)
                p.w("tensor", cS, w2_ready)
                for t8 in range(8):
                    sl = t8 % 2
                    p.w("tensor", cV, yadd_slot[sl])
                    for jc in range(8):
                        for fg in range(2):
                            p.e("tensor", lambda e, t8=t8, jc=jc, fg=fg, fh=fh, sl=sl: e.matmul(yps[sl][:, fg * 512:(fg + 1) * 512], hm[:, jc * 1024 + t8 * 128: jc * 1024 + (t8 + 1) * 128], w2b[fh][:, jc * 1024 + fg * 512: jc * 1024 + (fg + 1) * 512], start=(jc == 0), stop=(jc == 7)), inc=(cT if (jc == 7 and fg == 1) else None))
                    ymm = p.v(cT)
                    p.w("vector", cT, ymm)
                    if slab == 0:
                        if Hh == 1:
                            p.w("vector", cS, silu_mark)
                            p.dbar("vector", dG)
                        p.e("vector", lambda e, t8=t8, fh=fh, sl=sl: e.tensor_copy(y_acc[:, t8 * D + fh * 1024: t8 * D + (fh + 1) * 1024], yps[sl][:]), inc=cV)
                    else:
                        p.e("vector", lambda e, t8=t8, fh=fh, sl=sl: e.tensor_tensor(out=y_acc[:, t8 * D + fh * 1024: t8 * D + (fh + 1) * 1024], in0=y_acc[:, t8 * D + fh * 1024: t8 * D + (fh + 1) * 1024], in1=yps[sl][:], op=OP.add), inc=cV)
                    yadd_slot[sl] = p.v(cV)
                w2cons_fh[fh] = p.v(cT)
        yfin = p.v(cV)
        for t8 in range(8):
            p.w("scalar", cV, yfin)
            p.e("scalar", lambda e, t8=t8: e.activation(out=y_acc[:, t8 * D:(t8 + 1) * D], in_=y_acc[:, t8 * D:(t8 + 1) * D], func=AF.Silu), inc=cS)
            gdone = p.v(cS)
            p.w("gpsimd", cS, gdone)
            gst_dma[0] = p.dma("gpsimd", lambda e, t8=t8, tok0=tok0: e.dma_start(out=g_d[tok0 + t8 * 128: tok0 + (t8 + 1) * 128, :], in_=y_acc[:, t8 * D:(t8 + 1) * D]), dG)
        silu_mark = p.v(cS)
    p3.close()

    p2o = ExitStack()
    osb = p2o.enter_context(nc.sbuf_tensor("osb", [128, NH * L], BF16))
    p2 = ExitStack()
    wring2 = [p2.enter_context(nc.sbuf_tensor(f"wring2{i}", [128, 512], F32)) for i in range(2)]
    wqb = [p2.enter_context(nc.sbuf_tensor(f"wqb{i}", [128, D], BF16)) for i in range(2)]
    wub = [p2.enter_context(nc.sbuf_tensor(f"wub{i}", [128, D], BF16)) for i in range(2)]
    qtsb = [p2.enter_context(nc.sbuf_tensor(f"qtsb{i}", [128, L], BF16)) for i in range(2)]
    utsb = [p2.enter_context(nc.sbuf_tensor(f"utsb{i}", [128, L], BF16)) for i in range(2)]
    attnsb = [p2.enter_context(nc.sbuf_tensor(f"attnsb{i}", [128, L], F32)) for i in range(2)]
    sq32 = p2.enter_context(nc.sbuf_tensor("sq32", [128, 1024], F32))
    sqh = p2.enter_context(nc.sbuf_tensor("sqh", [128, 1024], BF16))
    sql = p2.enter_context(nc.sbuf_tensor("sql", [128, 1024], BF16))
    srow = p2.enter_context(nc.sbuf_tensor("srow", [1, 512], F32))
    srow2 = p2.enter_context(nc.sbuf_tensor("srow2", [1, 512], F32))
    tmp512 = p2.enter_context(nc.sbuf_tensor("tmp512", [128, 512], F32))
    qp = [p2.enter_context(nc.psum_tensor(f"qp{i}", [128, 512], F32)) for i in range(2)]
    ap_ = [p2.enter_context(nc.psum_tensor(f"ap{i}", [128, 512], F32)) for i in range(2)]
    ssp = p2.enter_context(nc.psum_tensor("ssp", [128, 512], F32))
    bcp = [p2.enter_context(nc.psum_tensor(f"bcp{i}", [128, 512], F32)) for i in range(2)]

    wq_cast = [0, 0]
    wq_cons = [0, 0]
    q_evac = {}
    u_evac = {}
    o_done = {}
    au_done = {}
    qu_done = {}
    wring2_slot = [0, 0]

    def stage_qu(h):
        par = h % 2
        first_cast = True
        for wdst, wsrc in ((wqb, wqr_d), (wub, wur_d)):
            for hf in range(4):
                sslot = hf % 2
                p.w("gpsimd", cS, wring2_slot[sslot])
                wdma = p.dma("gpsimd", lambda e, h=h, hf=hf, sslot=sslot, wsrc=wsrc: e.dma_start(out=wring2[sslot][:], in_=wsrc[h, :, hf * 512:(hf + 1) * 512]), dG)
                p.wd("scalar", wdma)
                if first_cast and h >= 2:
                    p.w("scalar", cT, wq_cons[par])
                first_cast = False
                p.e("scalar", lambda e, par=par, hf=hf, sslot=sslot, wdst=wdst: e.copy(out=wdst[par][:, hf * 512:(hf + 1) * 512], in_=wring2[sslot][:]), inc=cS)
                wring2_slot[sslot] = p.v(cS)
        wq_cast[par] = p.v(cS)
        p.w("tensor", cS, wq_cast[par])
        if h == 0:
            p.w("tensor", cV, kv_ready)
        for ng in range(4):
            sl = ng % 2
            if ng >= 2:
                p.w("tensor", cS, q_evac[(h, ng - 2)])
            elif h >= 1:
                p.w("tensor", cS, u_evac[(h - 1, ng + 2)])
            for i in range(NC):
                p.e("tensor", lambda e, h=h, i=i, ng=ng, par=par, sl=sl: e.matmul(qp[sl][:], wqb[par][:, i * 128:(i + 1) * 128], A[:, i * L + ng * 512:i * L + (ng + 1) * 512], start=(i == 0), stop=(i == NC - 1)), inc=(cT if i == NC - 1 else None))
            qmm = p.v(cT)
            p.w("scalar", cT, qmm)
            p.e("scalar", lambda e, h=h, ng=ng, sl=sl, par=par: e.activation(out=qtsb[par][:, ng * 512:(ng + 1) * 512], in_=qp[sl][:], func=AF.Relu, scale=INV_SCALE), inc=cS)
            q_evac[(h, ng)] = p.v(cS)
        for ng in range(4):
            sl = ng % 2
            p.w("tensor", cS, u_evac[(h, ng - 2)] if ng >= 2 else q_evac[(h, 3)])
            for i in range(NC):
                p.e("tensor", lambda e, h=h, i=i, ng=ng, par=par, sl=sl: e.matmul(qp[sl][:], wub[par][:, i * 128:(i + 1) * 128], A[:, i * L + ng * 512:i * L + (ng + 1) * 512], start=(i == 0), stop=(i == NC - 1)), inc=(cT if i == NC - 1 else None))
            umm = p.v(cT)
            p.w("scalar", cT, umm)
            if ng == 0 and h >= 2:
                p.w("scalar", cV, au_done[h - 2])
            p.e("scalar", lambda e, h=h, ng=ng, sl=sl, par=par: e.activation(out=utsb[par][:, ng * 512:(ng + 1) * 512], in_=qp[sl][:], func=AF.Silu), inc=cS)
            u_evac[(h, ng)] = p.v(cS)
        wq_cons[par] = p.v(cT)
        qu_done[h] = p.v(cS)

    def attn_mms(h):
        par = h % 2
        p.w("tensor", cS, qu_done[h])
        au_done[h] = None
        for ng in range(4):
            sl = ng % 2
            if ng >= 2:
                p.w("tensor", cV, attn_evac[ng - 2])
            p.e("tensor", lambda e, h=h, ng=ng, sl=sl, par=par: e.matmul(ap_[sl][:], kvsb[:, h * 128:(h + 1) * 128], qtsb[par][:, ng * 512:(ng + 1) * 512], start=True, stop=True), inc=cT)
            amm = p.v(cT)
            p.w("vector", cT, amm)
            p.e("vector", lambda e, ng=ng, sl=sl, par=par: e.tensor_copy(attnsb[par][:, ng * 512:(ng + 1) * 512], ap_[sl][:]), inc=cV)
            attn_evac[ng] = p.v(cV)
            p.e("vector", lambda e, h=h, ng=ng, par=par: e.tensor_tensor(out=osb[:, h * L + ng * 512: h * L + (ng + 1) * 512], in0=attnsb[par][:, ng * 512:(ng + 1) * 512], in1=utsb[par][:, ng * 512:(ng + 1) * 512], op=OP.mult), inc=cV)
        au_done[h] = p.v(cV)

    def ssbc(h):
        par = h % 2
        for hh2 in range(2):
          p.e("vector", lambda e, par=par, hh2=hh2: e.tensor_tensor(out=sq32[:], in0=attnsb[par][:, hh2 * 1024:(hh2 + 1) * 1024], in1=attnsb[par][:, hh2 * 1024:(hh2 + 1) * 1024], op=OP.mult), inc=cV)
          sq_done = p.v(cV)
          p.w("scalar", cV, sq_done)
          p.e("scalar", lambda e: e.copy(out=sqh[:], in_=sq32[:]), inc=cS)
          sqh_done = p.v(cS)
          p.w("vector", cS, sqh_done)
          p.e("vector", lambda e: e.tensor_tensor(out=sql[:], in0=sq32[:], in1=sqh[:], op=OP.subtract), inc=cV)
          sql_done = p.v(cV)
          p.w("tensor", cV, sql_done)
          for ng in (2 * hh2, 2 * hh2 + 1):
            sl = ng % 2
            lo = (ng % 2) * 512
            p.e("tensor", lambda e, lo=lo: e.matmul(ssp[0:1, 0:512], onescol[:], sqh[:, lo:lo + 512], start=True, stop=False))
            p.e("tensor", lambda e, lo=lo: e.matmul(ssp[0:1, 0:512], onescol[:], sql[:, lo:lo + 512], start=False, stop=True), inc=cT)
            ssmm = p.v(cT)
            p.w("vector", cT, ssmm)
            p.e("vector", lambda e, ng=ng: e.tensor_scalar(out=srow[0:1, 0:512], in0=ssp[0:1, 0:512], scalar1=1.0 / 128.0, scalar2=1e-24, op0=OP.mult, op1=OP.max), inc=cV)
            ts_done = p.v(cV)
            p.w("scalar", cV, ts_done)
            p.e("scalar", lambda e: e.activation(out=srow2[0:1, 0:512], in_=srow[0:1, 0:512], func=AF.Ln), inc=cS)
            p.e("scalar", lambda e: e.activation(out=srow[0:1, 0:512], in_=srow2[0:1, 0:512], func=AF.Exp, scale=-0.5), inc=cS)
            rec_done = p.v(cS)
            p.w("tensor", cS, rec_done)
            if ng >= 2:
                p.w("tensor", cV, omul1[ng - 2])
            p.e("tensor", lambda e, ng=ng, sl=sl: e.matmul(bcp[sl][:], onesrf[0:1, :], srow[0:1, 0:512], start=True, stop=True), inc=cT)
            bcmm = p.v(cT)
            p.w("vector", cT, bcmm)
            p.e("vector", lambda e, h=h, ng=ng, sl=sl: e.tensor_tensor(out=osb[:, h * L + ng * 512: h * L + (ng + 1) * 512], in0=osb[:, h * L + ng * 512: h * L + (ng + 1) * 512], in1=bcp[sl][:], op=OP.mult), inc=cV)
            omul1[ng] = p.v(cV)
        o_done[h] = p.v(cV)

    for h in range(NH):
        stage_qu(h)
        if h >= 1:
            attn_mms(h - 1)
        if h >= 2:
            ssbc(h - 2)
    attn_mms(NH - 1)
    ssbc(NH - 2)
    ssbc(NH - 1)
    p2_end_V = p.v(cV)
    p2_end_S = p.v(cS)
    p2_end_T = p.v(cT)
    p2.close()

    p4 = ExitStack()
    wob = p4.enter_context(nc.sbuf_tensor("wob", [128, NC * 512], BF16))
    woring = [p4.enter_context(nc.sbuf_tensor(f"woring{i}", [128, 512], F32)) for i in range(2)]
    xst = [p4.enter_context(nc.sbuf_tensor(f"xst{i}", [128, 512], F32)) for i in range(2)]
    gst = [p4.enter_context(nc.sbuf_tensor(f"gst{i}", [128, 512], F32)) for i in range(2)]
    sgt = [p4.enter_context(nc.sbuf_tensor(f"sgt{i}", [128, 512], F32)) for i in range(2)]
    ost = [p4.enter_context(nc.sbuf_tensor(f"ost{i}", [128, 512], F32)) for i in range(2)]
    sps = [p4.enter_context(nc.psum_tensor(f"sps{i}", [128, 512], F32)) for i in range(2)]

    wo_slot = [0, 0]
    sg_slot = [0, 0]
    gadd_slot = [0, 0]
    out_slot = [None, None]
    smm_last = 0
    p.w("gpsimd", cV, p2_end_V)
    p.w("gpsimd", cS, p2_end_S)
    p.w("gpsimd", cT, p2_end_T)
    for f in range(4):
        for i in range(NC):
            rs = i % 2
            p.w("gpsimd", cS, wo_slot[rs])
            dd = p.dma("gpsimd", lambda e, i=i, f=f, rs=rs: e.dma_start(out=woring[rs][:], in_=wo_d[i * 128:(i + 1) * 128, f * 512:(f + 1) * 512]), dG)
            p.wd("scalar", dd)
            if f > 0 and i == 0:
                p.w("scalar", cT, smm_last)
            p.e("scalar", lambda e, i=i, rs=rs: e.copy(out=wob[:, i * 512:(i + 1) * 512], in_=woring[rs][:]), inc=cS)
            wo_slot[rs] = p.v(cS)
        wo_ready = p.v(cS)
        p.w("tensor", cS, wo_ready)
        for t in range(NT):
            sl = t % 2
            p.w("gpsimd", cG, gadd_slot[sl])
            p.w("gpsimd", cV, sg_slot[sl])
            if t == 0 and f == 0:
                p.dbar("gpsimd", dG)
            xtok = p.dma("gpsimd", lambda e, t=t, f=f, sl=sl: e.dma_start(out=xst[sl][:], in_=x_d[t * 128:(t + 1) * 128, f * 512:(f + 1) * 512]), dG)
            gtok = p.dma("gpsimd", lambda e, t=t, f=f, sl=sl: e.dma_start(out=gst[sl][:], in_=g_d[t * 128:(t + 1) * 128, f * 512:(f + 1) * 512]), dG)
            p.w("tensor", cV, sg_slot[sl])
            for i in range(NC):
                p.e("tensor", lambda e, i=i, t=t, sl=sl: e.matmul(sps[sl][:], osb[:, i * L + t * 128: i * L + (t + 1) * 128], wob[:, i * 512:(i + 1) * 512], start=(i == 0), stop=(i == NC - 1)), inc=(cT if i == NC - 1 else None))
            smm = p.v(cT)
            smm_last = smm
            p.w("vector", cT, smm)
            p.wd("vector", xtok)
            p.wd("vector", gtok)
            p.w("vector", cG, gadd_slot[sl])
            p.e("vector", lambda e, t=t, sl=sl: e.tensor_tensor(out=sgt[sl][:], in0=sps[sl][:], in1=gst[sl][:], op=OP.mult), inc=cV)
            sg_slot[sl] = p.v(cV)
            p.w("gpsimd", cV, sg_slot[sl])
            p.wd("gpsimd", out_slot[sl])
            p.e("gpsimd", lambda e, t=t, sl=sl: e.tensor_tensor(out=ost[sl][:], in0=sgt[sl][:], in1=xst[sl][:], op=OP.add), inc=cG)
            gadd_slot[sl] = p.v(cG)
            p.w("sync", cG, gadd_slot[sl])
            out_slot[sl] = p.dma("sync", lambda e, t=t, f=f, sl=sl: e.dma_start(out=out_d[t * 128:(t + 1) * 128, f * 512:(f + 1) * 512], in_=ost[sl][:]), dO)
    p4.close()
    p2o.close()

    p.dbar("sync", dO)

    with nc.Block() as blk:
        @blk.tensor
        def _(e):
            for f in p.st["tensor"]:
                f(e)

        @blk.vector
        def _(e):
            for f in p.st["vector"]:
                f(e)

        @blk.scalar
        def _(e):
            for f in p.st["scalar"]:
                f(e)

        @blk.gpsimd
        def _(e):
            for f in p.st["gpsimd"]:
                f(e)

        @blk.sync
        def _(e):
            for f in p.st["sync"]:
                f(e)
    es.close()
    return nc


wring_cast = [0, 0, 0, 0]
attn_evac = [0, 0, 0, 0]
omul1 = [0, 0, 0, 0]


def kernel(L0_attn_x, L0_attn_Wq, L0_attn_Wk, L0_attn_Wv, L0_attn_Wu, L0_attn_Wo,
           L0_ffn_Wr, L0_ffn_br, L0_ffn_W1, L0_ffn_W2):
    X = np.ascontiguousarray(L0_attn_x, dtype=np.float32)
    wq = np.ascontiguousarray(L0_attn_Wq, dtype=np.float32)
    wk = np.ascontiguousarray(L0_attn_Wk, dtype=np.float32)
    wv = np.ascontiguousarray(L0_attn_Wv, dtype=np.float32)
    wu = np.ascontiguousarray(L0_attn_Wu, dtype=np.float32)
    wo = np.ascontiguousarray(L0_attn_Wo, dtype=np.float32)
    wr = np.ascontiguousarray(L0_ffn_Wr, dtype=np.float32)
    br = np.ascontiguousarray(L0_ffn_br, dtype=np.float32)
    w1 = np.ascontiguousarray(L0_ffn_W1, dtype=np.float32)
    w2 = np.ascontiguousarray(L0_ffn_W2, dtype=np.float32)

    wqr = np.ascontiguousarray(wq.reshape(NC, 128, NH, 128).transpose(2, 1, 0, 3).reshape(NH, 128, D))
    wur = np.ascontiguousarray(wu.reshape(NC, 128, NH, 128).transpose(2, 1, 0, 3).reshape(NH, 128, D))
    w1t = np.ascontiguousarray(w1.T)
    w1r = np.ascontiguousarray(w1t.reshape(NC, 128, 64, 128).transpose(2, 1, 0, 3).reshape(64, 128, D))
    w2t = np.ascontiguousarray(w2.T)
    wrt = np.ascontiguousarray(wr.T)
    wrr = np.ascontiguousarray(wrt.reshape(NC, 128, NB).transpose(1, 0, 2).reshape(128, NC * NB))
    brv = np.ascontiguousarray(br.reshape(1, NB))
    idnbf = np.eye(128, dtype=bfloat16)
    onescol = np.ones((128, 1), dtype=bfloat16)
    onesrb = np.ones((1, 128), dtype=bfloat16)
    onesrf = np.ones((1, 128), dtype=np.float32)

    nc = bass.Bass("TRN2", target_bir_lowering=False)
    build(nc)

    shared = dict(wk=wk, wv=wv, wo=wo, wqr=wqr, wur=wur, w1r=w1r, w2t=w2t,
                  wrr=wrr, brd=brv, idnbf=idnbf, onescol=onescol,
                  onesrb=onesrb, onesrf=onesrf,
                  cst=np.tile(np.array([[1e-5, INV_SCALE]], dtype=np.float32), (128, 1)))
    in_maps = [dict(shared, x=np.ascontiguousarray(X[b])) for b in range(8)]
    res = run_bass_kernel_spmd(nc, in_maps, core_ids=list(range(8)))
    return np.stack([np.asarray(r["out"], dtype=np.float32) for r in res.results], axis=0)
